# version 03
# baseline (speedup 1.0000x reference)
import numpy as np
import ml_dtypes
from contextlib import ExitStack

import concourse.bass as bass
import concourse.mybir as mybir
from concourse.bass_utils import run_bass_kernel_spmd

F32 = mybir.dt.float32
BF16 = mybir.dt.bfloat16
AF = mybir.ActivationFunctionType
ALU = mybir.AluOpType
AX = mybir.AxisListType

NCORES = 8
S = 2048
D = 1024
H = 16
HD = 64
HPAIRS = 8
DFF = 4096
NT = S // 128
NCH = S // 512
DC = D // 128
FC = DFF // 128
EPS = 1e-5
SCALE = 0.125

ENGINES = ("tensor", "vector", "scalar", "gpsimd", "sync")


class Sem:
    def __init__(self, nc, stack, name):
        self.h = stack.enter_context(nc.semaphore(name))
        self.n = 0
        self.name = name


class Prog:
    def __init__(self):
        self.ops = {e: [] for e in ENGINES}

    def emit(self, eng, fn, wait=(), inc=None, inc_n=1):
        cnt = None
        if inc is not None:
            inc.n += inc_n
            cnt = inc.n
        w = tuple((s, c) for (s, c) in wait if s is not None and c)
        self.ops[eng].append((w, fn, inc, inc_n))
        return cnt

    def realize(self, block):
        for eng in ENGINES:
            ops = self.ops[eng]
            if not ops:
                continue

            def body(E, _ops=ops):
                for w, fn, inc, inc_n in _ops:
                    for (sem, cnt) in w:
                        E.wait_ge(sem.h, cnt)
                    instr = fn(E)
                    if inc is not None:
                        instr.then_inc(inc.h, inc_n)

            getattr(block, eng)(body)


def build_core_program(nseq=4):
    nc = bass.Bass("TRN2", target_bir_lowering=False)

    x_in = nc.dram_tensor("x", [nseq, S, D], F32, kind="ExternalInput").ap()
    w_qkv = nc.dram_tensor("w_qkv", [D, 3 * D], F32, kind="ExternalInput").ap()
    w_out = nc.dram_tensor("w_out", [D, D], F32, kind="ExternalInput").ap()
    W1 = nc.dram_tensor("W1", [DFF, D], F32, kind="ExternalInput").ap()
    W2 = nc.dram_tensor("W2", [D, DFF], F32, kind="ExternalInput").ap()
    bqkv_blk = nc.dram_tensor("bqkv_blk", [24, 128], F32, kind="ExternalInput").ap()
    bqkv_row = nc.dram_tensor("bqkv_row", [1, 3 * D], F32, kind="ExternalInput").ap()
    b1_blk = nc.dram_tensor("b1_blk", [32, 128], F32, kind="ExternalInput").ap()
    bout_row = nc.dram_tensor("bout_row", [1, D], F32, kind="ExternalInput").ap()
    b2_row = nc.dram_tensor("b2_row", [1, D], F32, kind="ExternalInput").ap()
    cmask_in = nc.dram_tensor("cmask", [128, 4 * 512], BF16, kind="ExternalInput").ap()
    id32_in = nc.dram_tensor("id32", [128, 128], F32, kind="ExternalInput").ap()
    id16_in = nc.dram_tensor("id16", [128, 128], BF16, kind="ExternalInput").ap()

    wq1t_s = nc.dram_tensor("wq1t_s", [128, DC * DFF], BF16, kind="Internal").ap()
    wq2t_s = nc.dram_tensor("wq2t_s", [128, FC * D], BF16, kind="Internal").ap()
    y_s = nc.dram_tensor("y_s", [S, D], F32, kind="Internal").ap()

    out = nc.dram_tensor("out", [nseq, S, D], F32, kind="ExternalOutput").ap()

    P = Prog()

    with ExitStack() as top:
        et = top.enter_context

        cmask = et(nc.sbuf_tensor("cmask_sb", [128, 4 * 512], BF16))
        id32 = et(nc.sbuf_tensor("id32_sb", [128, 128], F32))
        id16 = et(nc.sbuf_tensor("id16_sb", [128, 128], BF16))
        ones16 = et(nc.sbuf_tensor("ones16", [1, 128], BF16))
        onesc32 = et(nc.sbuf_tensor("onesc32", [128, 1], F32))
        ones32 = et(nc.sbuf_tensor("ones32", [1, 128], F32))
        bqkvT = et(nc.sbuf_tensor("bqkvT", [128, 24], F32))
        b1T = et(nc.sbuf_tensor("b1T", [128, 32], F32))
        bv16 = et(nc.sbuf_tensor("bv16", [1, D], BF16))
        bout16 = et(nc.sbuf_tensor("bout16", [1, D], BF16))
        b2_16 = et(nc.sbuf_tensor("b2_16", [1, D], BF16))
        sca = et(nc.sbuf_tensor("sca", [128, 24], F32))

        sems = {}
        for name in ("te", "ve", "se", "ldc", "ldx0", "ldx1", "lW", "st_wq",
                     "g_wqkv", "g_wout", "g_wq", "g_ytm", "ldr0", "ldr1",
                     "st_y", "st_out"):
            sems[name] = Sem(nc, top, name)
        te, ve, se = sems["te"], sems["ve"], sems["se"]
        ldc = sems["ldc"]
        ldx = (sems["ldx0"], sems["ldx1"])
        lW = sems["lW"]
        st_wq = sems["st_wq"]
        g_wqkv, g_wout, g_wq, g_ytm = (sems["g_wqkv"], sems["g_wout"],
                                       sems["g_wq"], sems["g_ytm"])
        ldr = (sems["ldr0"], sems["ldr1"])
        st_y, st_out = sems["st_y"], sems["st_out"]

        E = P.emit

        with ExitStack() as ist:
            ei = ist.enter_context
            btmp = ei(nc.sbuf_tensor("btmp", [32, 128], F32))
            brow = ei(nc.sbuf_tensor("brow", [1, 3 * D], F32))
            ps_i = ei(nc.psum_tensor("ps_i", [128, 64], F32))

            for src, dst in ((cmask_in, cmask), (id32_in, id32), (id16_in, id16)):
                E("sync", lambda s, a=src, b=dst: s.dma_start(out=b[:], in_=a),
                  inc=ldc, inc_n=16)
            E("sync", lambda s: s.dma_start(out=brow[:], in_=bqkv_row),
              inc=ldc, inc_n=16)
            c_all = ldc.n

            E("vector", lambda v: v.memset(ones16[:], 1.0))
            E("vector", lambda v: v.memset(onesc32[:], 1.0))
            c_ones = E("vector", lambda v: v.memset(ones32[:], 1.0), inc=ve)
            c_bv = E("vector", lambda v: v.tensor_copy(out=bv16[:], in_=brow[0:1, 2 * D:3 * D]),
                     wait=[(ldc, c_all)], inc=ve)

            E("sync", lambda s: s.dma_start(out=btmp[0:24, :], in_=bqkv_blk),
              wait=[(ve, c_bv)], inc=ldc, inc_n=16)
            c_b1 = ldc.n
            c_t1 = E("tensor", lambda t: t.transpose(out=ps_i[:, 0:24], in_=btmp[0:24, :], identity=id32[0:24, 0:24]),
                     wait=[(ldc, c_b1), (ve, c_ones)], inc=te)
            c_e1 = E("vector", lambda v: v.tensor_copy(out=bqkvT[:], in_=ps_i[:, 0:24]),
                     wait=[(te, c_t1)], inc=ve)
            E("sync", lambda s: s.dma_start(out=btmp[0:32, :], in_=b1_blk),
              wait=[(ve, c_e1)], inc=ldc, inc_n=16)
            c_b2 = ldc.n
            c_t2 = E("tensor", lambda t: t.transpose(out=ps_i[:, 32:64], in_=btmp[0:32, :], identity=id32[0:32, 0:32]),
                     wait=[(ldc, c_b2)], inc=te)
            c_e2 = E("vector", lambda v: v.tensor_copy(out=b1T[:], in_=ps_i[:, 32:64]),
                     wait=[(te, c_t2)], inc=ve)
            E("sync", lambda s: s.dma_start(out=brow[0:1, 0:D], in_=bout_row),
              wait=[(ve, c_e2)], inc=ldc, inc_n=16)
            c_bo = E("vector", lambda v: v.tensor_copy(out=bout16[:], in_=brow[0:1, 0:D]),
                     wait=[(ldc, ldc.n)], inc=ve)
            E("sync", lambda s: s.dma_start(out=brow[0:1, 0:D], in_=b2_row),
              wait=[(ve, c_bo)], inc=ldc, inc_n=16)
            c_binit = E("vector", lambda v: v.tensor_copy(out=b2_16[:], in_=brow[0:1, 0:D]),
                        wait=[(ldc, ldc.n)], inc=ve)

        if True:
            def quantize(widx, W_hbm, nchunks_in, in_chunk, out_cols, scratch,
                         thr_col, nthr_col, extra_wait):
                with ExitStack() as qs:
                    eq = qs.enter_context
                    wf = eq(nc.sbuf_tensor(f"wf{widx}", [128, nchunks_in * in_chunk], F32))
                    stg = eq(nc.sbuf_tensor(f"stg{widx}", [128, out_cols], BF16))
                    red = eq(nc.sbuf_tensor(f"red{widx}", [128, 256], F32))
                    nbuf = eq(nc.sbuf_tensor(f"nbuf{widx}", [128, 128], F32))
                    ps_q = eq(nc.psum_tensor(f"ps_q{widx}", [128, 128], F32))
                    ps_tr = [eq(nc.psum_tensor(f"ps_tq{k}_{widx}", [128, 128], F32))
                             for k in range(2)]

                    c_w = E("sync", lambda s: s.dma_start(
                        out=wf[:].rearrange("p (c n) -> p c n", c=nchunks_in),
                        in_=W_hbm.rearrange("(c p) n -> p c n", p=128)),
                        wait=[(st_wq, st_wq.n)] + list(extra_wait),
                        inc=lW, inc_n=16)

                    nel = 128 * nchunks_in * in_chunk
                    c_r1 = E("vector", lambda v: v.tensor_reduce(
                        out=red[:], in_=wf[:].rearrange("p (a b) -> p a b", a=256),
                        axis=AX.X, op=ALU.add, apply_absolute_value=True),
                        wait=[(lW, c_w)], inc=ve)
                    c_r2 = E("vector", lambda v: v.tensor_reduce(
                        out=red[:, 0:1], in_=red[:], axis=AX.X, op=ALU.add,
                        apply_absolute_value=True),
                        wait=[(ve, c_r1)], inc=ve)
                    c_m1 = E("tensor", lambda t: t.matmul(
                        out=ps_q[0:1, 0:1], lhsT=onesc32[:], rhs=red[:, 0:1],
                        start=True, stop=True, skip_group_check=True),
                        wait=[(ve, c_r2)], inc=te)
                    c_g = E("vector", lambda v: v.tensor_copy(
                        out=sca[0:1, 20:21], in_=ps_q[0:1, 0:1]),
                        wait=[(te, c_m1)], inc=ve)
                    c_m2 = E("tensor", lambda t: t.matmul(
                        out=ps_q[:, 0:1], lhsT=ones32[:], rhs=sca[0:1, 20:21],
                        start=True, stop=True, skip_group_check=True),
                        wait=[(ve, c_g)], inc=te)
                    c_thr = E("vector", lambda v: v.tensor_scalar(
                        out=sca[:, thr_col:thr_col + 1], in0=ps_q[:, 0:1],
                        scalar1=0.5 / nel, scalar2=0.5 * EPS, op0=ALU.mult, op1=ALU.add),
                        wait=[(te, c_m2)], inc=ve)
                    c_nthr = E("vector", lambda v: v.tensor_scalar(
                        out=sca[:, nthr_col:nthr_col + 1], in0=sca[:, thr_col:thr_col + 1],
                        scalar1=-1.0, scalar2=None, op0=ALU.mult),
                        wait=[(ve, c_thr)], inc=ve)

                    nblk_out = DC if widx == 1 else FC
                    nblk_in = FC if widx == 1 else DC
                    last_stt = c_nthr
                    prev_stt = {}
                    for bi in range(nblk_in):
                        for bo in range(nblk_out):
                            par = (bi * nblk_out + bo) % 2
                            src = wf[:, bi * in_chunk + bo * 128: bi * in_chunk + bo * 128 + 128]
                            c_tp = E("tensor", lambda t, s_=src, p_=ps_tr[par]: t.transpose(
                                out=p_[:], in_=s_, identity=id32[:]),
                                wait=[(lW, c_w), (ve, c_nthr),
                                      (ve, prev_stt.get(par))], inc=te)
                            c_n = E("vector", lambda v, p_=ps_tr[par]: v.tensor_scalar(
                                out=nbuf[:], in0=p_[:],
                                scalar1=sca[:, nthr_col:nthr_col + 1], scalar2=None,
                                op0=ALU.is_lt),
                                wait=[(te, c_tp), (ve, last_stt)], inc=ve)
                            if widx == 1:
                                dst = stg[:, bo * DFF + bi * 128: bo * DFF + bi * 128 + 128]
                            else:
                                dst = stg[:, bo * D + bi * 128: bo * D + bi * 128 + 128]
                            c_stt = E("vector", lambda v, p_=ps_tr[par], d_=dst: v.scalar_tensor_tensor(
                                out=d_, in0=p_[:], scalar=sca[:, thr_col:thr_col + 1],
                                in1=nbuf[:], op0=ALU.is_gt, op1=ALU.subtract),
                                wait=[(ve, c_n)], inc=ve)
                            prev_stt[par] = c_stt
                            last_stt = c_stt
                    E("sync", lambda s: s.dma_start(out=scratch, in_=stg[:]),
                      wait=[(ve, last_stt)], inc=st_wq, inc_n=16)

            quantize(1, W1, FC, D, DC * DFF, wq1t_s, 0, 1, [(ve, c_binit)])
            quantize(2, W2, DC, DFF, FC * D, wq2t_s, 2, 3, [])
            quant_done_st = st_wq.n

        prev_end = {"ve": ve.n, "se": se.n, "te": te.n, "st_wq": quant_done_st}

        for s_ in range(nseq):
            sfx = f"_{s_}"
            with ExitStack() as sab:
                ea = sab.enter_context
                xfm = ea(nc.sbuf_tensor("xfm" + sfx, [128, DC * S], BF16))
                Qf = ea(nc.sbuf_tensor("Qf" + sfx, [128, DC * S], BF16))
                Kf = ea(nc.sbuf_tensor("Kf" + sfx, [128, DC * S], BF16))
                Vaug = ea(nc.sbuf_tensor("Vaug" + sfx, [128, NT * 1040], BF16))

                with ExitStack() as sa:
                    e2 = sa.enter_context
                    wq16 = e2(nc.sbuf_tensor("wq16" + sfx, [128, DC * 3 * D], BF16))
                    xin = [e2(nc.sbuf_tensor(f"xin{s_}_{k}", [128, D], F32)) for k in range(2)]
                    ps_xt = [e2(nc.psum_tensor(f"ps_xt{s_}_{k}", [128, D], F32)) for k in range(2)]
                    ps_qk = [e2(nc.psum_tensor(f"ps_qk{s_}_{k}", [128, 512], F32)) for k in range(2)]
                    ps_v = [e2(nc.psum_tensor(f"ps_v{s_}_{k}", [128, 512], F32)) for k in range(2)]

                    for dcw in range(DC):
                        E("gpsimd", lambda g, d_=dcw: g.dma_start(
                            out=wq16[:, d_ * 3 * D:(d_ + 1) * 3 * D],
                            in_=w_qkv[d_ * 128:(d_ + 1) * 128, :],
                            max_dma_last_dim=8192),
                            wait=([(ve, prev_end["ve"]), (se, prev_end["se"]),
                                   (te, prev_end["te"]), (st_wq, prev_end["st_wq"])]
                                  if dcw == 0 else ()),
                            inc=g_wqkv, inc_n=16)
                    wqkv_ready = g_wqkv.n

                    v_ready = []
                    xfm_ready = []
                    c_evac = {}
                    c_trans = {}
                    c_vevac = {}
                    for it in range(NT):
                        par = it % 2
                        wl_load = [(te, c_trans.get(it - 2))]
                        if it < 2:
                            wl_load += [(ve, prev_end["ve"]), (se, prev_end["se"]),
                                        (te, prev_end["te"]),
                                        (st_wq, prev_end["st_wq"])]
                        E("sync", lambda s, it_=it, p_=par, sq=s_: s.dma_start(
                            out=xin[p_][:], in_=x_in[sq, it_ * 128:(it_ + 1) * 128, :]),
                            wait=wl_load, inc=ldx[par], inc_n=16)
                        nld = ldx[par].n
                        c_t = None
                        for dc in range(DC):
                            wl_t = []
                            if dc == 0:
                                wl_t = [(ldx[par], nld), (ve, c_evac.get(it - 2))]
                                if it < 2:
                                    wl_t.append((ve, prev_end["ve"]))
                            c_t = E("tensor", lambda t, p_=par, d_=dc: t.transpose(
                                out=ps_xt[p_][:, d_ * 128:(d_ + 1) * 128],
                                in_=xin[p_][:, d_ * 128:(d_ + 1) * 128],
                                identity=id32[:]),
                                wait=wl_t, inc=te if dc == DC - 1 else None)
                        c_trans[it] = c_t
                        c_e = E("vector", lambda v, p_=par, it_=it: v.tensor_copy(
                            out=xfm[:].rearrange("p (c n) -> p c n", c=DC)[:, :, it_ * 128:(it_ + 1) * 128],
                            in_=ps_xt[p_][:].rearrange("p (c n) -> p c n", c=DC)),
                            wait=[(te, c_trans[it])], inc=ve)
                        c_evac[it] = c_e
                        xfm_ready.append(c_e)
                        E("vector", lambda v, it_=it: v.memset(
                            Vaug[:, it_ * 1040:(it_ + 1) * 1040].rearrange(
                                "p (g c) -> p g c", c=65)[:, :, 64:65], 1.0))
                        for vc in range(2):
                            vpar = (it * 2 + vc) % 2
                            E("tensor", lambda t, p_=vpar, v_=vc: t.matmul(
                                out=ps_v[p_][:], lhsT=ones16[:],
                                rhs=bv16[0:1, v_ * 512:(v_ + 1) * 512],
                                start=True, stop=False, skip_group_check=True),
                                wait=[(ve, c_e), (g_wqkv, wqkv_ready),
                                      (ve, c_vevac.get((it * 2 + vc) - 2))])
                            c_mm = None
                            for dc in range(DC):
                                c_mm = E("tensor", lambda t, p_=vpar, v_=vc, d_=dc, it_=it: t.matmul(
                                    out=ps_v[p_][:],
                                    lhsT=xfm[:, d_ * S + it_ * 128: d_ * S + it_ * 128 + 128],
                                    rhs=wq16[:, d_ * 3 * D + 2 * D + v_ * 512: d_ * 3 * D + 2 * D + (v_ + 1) * 512],
                                    start=False, stop=(d_ == DC - 1), skip_group_check=True),
                                    inc=te if dc == DC - 1 else None)
                            c_vv = E("vector", lambda v, p_=vpar, v2=vc, it_=it: v.tensor_copy(
                                out=Vaug[:, it_ * 1040 + v2 * 8 * 65: it_ * 1040 + (v2 * 8 + 8) * 65].rearrange(
                                    "p (g c) -> p g c", c=65)[:, :, 0:64],
                                in_=ps_v[p_][:].rearrange("p (g c) -> p g c", c=64)),
                                wait=[(te, c_mm)], inc=ve)
                            c_vevac[it * 2 + vc] = c_vv
                        v_ready.append(c_vevac[it * 2 + 1])

                    qk_ready = []
                    c_qkevac = {}
                    for ch in range(NCH):
                        for qc in range(16):
                            qpar = (ch * 16 + qc) % 2
                            c_mm = None
                            for dc in range(DC):
                                wl_q = ()
                                if dc == 0:
                                    wl_q = [(ve, xfm_ready[4 * ch + 3]),
                                            (g_wqkv, wqkv_ready),
                                            (se, c_qkevac.get((ch * 16 + qc) - 2))]
                                c_mm = E("tensor", lambda t, p_=qpar, q_=qc, d_=dc, c_=ch: t.matmul(
                                    out=ps_qk[p_][:],
                                    lhsT=wq16[:, d_ * 3 * D + q_ * 128: d_ * 3 * D + (q_ + 1) * 128],
                                    rhs=xfm[:, d_ * S + c_ * 512: d_ * S + (c_ + 1) * 512],
                                    start=(d_ == 0), stop=(d_ == DC - 1), skip_group_check=True),
                                    wait=wl_q, inc=te if dc == DC - 1 else None)
                            dstbuf = Qf if qc < 8 else Kf
                            dcol = (qc % 8) * S + ch * 512
                            c_sev = E("scalar", lambda sc, p_=qpar, b_=dstbuf, col=dcol, q_=qc: sc.activation(
                                out=b_[:, col:col + 512], in_=ps_qk[p_][:],
                                func=AF.Identity, bias=bqkvT[:, q_:q_ + 1]),
                                wait=[(te, c_mm)], inc=se)
                            c_qkevac[ch * 16 + qc] = c_sev
                        qk_ready.append(c_qkevac[ch * 16 + 15])
                    phaseA = {"ve": ve.n, "se": se.n, "te": te.n}

                with ExitStack() as sb:
                    eb = sb.enter_context
                    Ofm = eb(nc.sbuf_tensor("Ofm" + sfx, [128, HPAIRS * S], BF16))
                    w_out16 = eb(nc.sbuf_tensor("wout16" + sfx, [128, DC * D], BF16))

                    E("gpsimd", lambda g: g.dma_start(
                        out=w_out16[:].rearrange("p (c n) -> p c n", c=DC),
                        in_=w_out.rearrange("(c p) n -> p c n", p=128)),
                        wait=[(se, phaseA["se"]), (te, phaseA["te"]),
                              (ve, phaseA["ve"])],
                        inc=g_wout, inc_n=16)
                    wout_ready = g_wout.n

                    with ExitStack() as sbp:
                        ep = sbp.enter_context
                        Pb = [ep(nc.sbuf_tensor(f"Pb{s_}_{k}", [128, 512], BF16)) for k in range(4)]
                        Pd = [ep(nc.sbuf_tensor(f"Pd{s_}_{k}", [128, 512], BF16)) for k in range(2)]
                        of32 = [ep(nc.sbuf_tensor(f"of32_{s_}_{k}", [128, 512], F32)) for k in range(2)]
                        otm = ep(nc.sbuf_tensor("otm" + sfx, [128, 512], BF16))
                        ps_s = [ep(nc.psum_tensor(f"ps_s{s_}_{k}", [128, 512], F32)) for k in range(4)]
                        ps_oe = ep(nc.psum_tensor("ps_oe" + sfx, [128, 512], F32))
                        ps_oo = ep(nc.psum_tensor("ps_oo" + sfx, [128, 512], F32))
                        ps_me = ep(nc.psum_tensor("ps_me" + sfx, [128, 512], F32))
                        ps_mo = ep(nc.psum_tensor("ps_mo" + sfx, [128, 512], F32))
                        tb_e = ps_me[:].bitcast(BF16)
                        tb_o = ps_mo[:].bitcast(BF16)

                        c_exp = {}
                        c_pv = {}
                        prev_norm = None
                        prev_last_pv = None
                        for hp in range(HPAIRS):
                            for ch in range(NCH):
                                jt0 = 4 * ch
                                for jt in range(jt0, NT):
                                    jdx = jt - jt0
                                    sl = jdx % 2
                                    wl = [(se, qk_ready[ch]), (ve, v_ready[jt])]
                                    if jdx >= 2:
                                        wl.append((se, c_exp[(hp, ch, jt - 2)]))
                                    elif prev_norm is not None:
                                        wl += [(ve, prev_norm["vofm"]),
                                               (se, prev_norm["se"])]
                                    else:
                                        wl += [(te, phaseA["te"]), (ve, phaseA["ve"]),
                                               (se, phaseA["se"])]
                                    E("tensor", lambda t, h_=hp, c_=ch, j_=jt, s0=sl: t.matmul(
                                        out=ps_s[2 * s0][:],
                                        lhsT=Kf[0:64, h_ * S + j_ * 128: h_ * S + (j_ + 1) * 128],
                                        rhs=Qf[0:64, h_ * S + c_ * 512: h_ * S + (c_ + 1) * 512],
                                        start=True, stop=True, skip_group_check=True,
                                        tile_position=(0, 0)),
                                        wait=wl)
                                    c_s = E("tensor", lambda t, h_=hp, c_=ch, j_=jt, s0=sl: t.matmul(
                                        out=ps_s[2 * s0 + 1][:],
                                        lhsT=Kf[64:128, h_ * S + j_ * 128: h_ * S + (j_ + 1) * 128],
                                        rhs=Qf[64:128, h_ * S + c_ * 512: h_ * S + (c_ + 1) * 512],
                                        start=True, stop=True, skip_group_check=True,
                                        tile_position=(64, 0)),
                                        inc=te)
                                    wl2 = [(te, c_s)]
                                    if jdx >= 2:
                                        wl2.append((te, c_pv[(hp, ch, jt - 2)]))
                                    elif prev_last_pv is not None:
                                        wl2.append((te, prev_last_pv))
                                    E("scalar", lambda sc, s0=sl: sc.activation(
                                        out=Pb[2 * s0][:], in_=ps_s[2 * s0][:],
                                        func=AF.Exp, scale=SCALE),
                                        wait=wl2)
                                    c_e2 = E("scalar", lambda sc, s0=sl: sc.activation(
                                        out=Pb[2 * s0 + 1][:], in_=ps_s[2 * s0 + 1][:],
                                        func=AF.Exp, scale=SCALE),
                                        inc=se)
                                    c_exp[(hp, ch, jt)] = c_e2
                                    diag = jdx < 4
                                    if diag:
                                        k = jdx
                                        wl_m = [(se, c_e2)]
                                        prev_pd = c_pv.get((hp, ch, jt - 1)) if jdx > 0 else prev_last_pv
                                        if prev_pd is not None:
                                            wl_m.append((te, prev_pd))
                                        E("vector", lambda v, s0=sl, k_=k: v.tensor_tensor(
                                            out=Pd[0][:], in0=Pb[2 * s0][:],
                                            in1=cmask[:, k_ * 512:(k_ + 1) * 512], op=ALU.mult),
                                            wait=wl_m)
                                        c_m1 = E("vector", lambda v, s0=sl, k_=k: v.tensor_tensor(
                                            out=Pd[1][:], in0=Pb[2 * s0 + 1][:],
                                            in1=cmask[:, k_ * 512:(k_ + 1) * 512], op=ALU.mult),
                                            inc=ve)
                                        src_e, src_o = Pd[0], Pd[1]
                                        wl3 = [(ve, c_m1)]
                                    else:
                                        src_e, src_o = Pb[2 * sl], Pb[2 * sl + 1]
                                        wl3 = [(se, c_e2)]
                                    E("tensor", lambda t, h_=hp, j_=jt, se_=src_e, f_=(jt == jt0), l_=(jt == NT - 1): t.matmul(
                                        out=ps_oe[0:65, :],
                                        lhsT=Vaug[:, j_ * 1040 + (2 * h_) * 65: j_ * 1040 + (2 * h_ + 1) * 65],
                                        rhs=se_[:], start=f_, stop=l_, skip_group_check=True),
                                        wait=wl3)
                                    c_p = E("tensor", lambda t, h_=hp, j_=jt, so_=src_o, f_=(jt == jt0), l_=(jt == NT - 1): t.matmul(
                                        out=ps_oo[0:65, :],
                                        lhsT=Vaug[:, j_ * 1040 + (2 * h_ + 1) * 65: j_ * 1040 + (2 * h_ + 2) * 65],
                                        rhs=so_[:], start=f_, stop=l_, skip_group_check=True),
                                        inc=te)
                                    c_pv[(hp, ch, jt)] = c_p

                                c_last_pv = c_pv[(hp, ch, NT - 1)]
                                prev_last_pv = c_last_pv
                                wl_c = [(te, c_last_pv)]
                                if prev_norm is not None:
                                    wl_c.append((te, prev_norm["te_tt"]))
                                E("vector", lambda v: v.tensor_copy(
                                    out=of32[0][0:65, :], in_=ps_oe[0:65, :]),
                                    wait=wl_c)
                                c_c1 = E("vector", lambda v: v.tensor_copy(
                                    out=of32[1][0:65, :], in_=ps_oo[0:65, :]), inc=ve)
                                c_tt = None
                                for head in range(2):
                                    pt = ps_me if head == 0 else ps_mo
                                    o32 = of32[head]
                                    for sub in range(4):
                                        wl_tt = ()
                                        if head == 0 and sub == 0:
                                            wl_tt = [(ve, c_c1)]
                                            if prev_norm is not None:
                                                wl_tt += [(se, prev_norm["se"]),
                                                          (ve, prev_norm["vofm"])]
                                        c_tt = E("tensor", lambda t, p_=pt, o_=o32, sb_=sub: t.transpose(
                                            out=p_[:, sb_ * 65:(sb_ + 1) * 65],
                                            in_=o_[0:65, sb_ * 128:(sb_ + 1) * 128],
                                            identity=id32[0:65, 0:65]),
                                            wait=wl_tt,
                                            inc=te if (head == 1 and sub == 3) else None)
                                c_r0 = E("vector", lambda v: v.reciprocal(
                                    out=sca[:, 8:12],
                                    in_=ps_me[:, 0:260].rearrange("p (s c) -> p s c", c=65)[:, :, 64:65]),
                                    wait=[(te, c_tt)], inc=ve)
                                c_r1 = E("vector", lambda v: v.reciprocal(
                                    out=sca[:, 12:16],
                                    in_=ps_mo[:, 0:260].rearrange("p (s c) -> p s c", c=65)[:, :, 64:65]),
                                    wait=[(ve, c_r0)], inc=ve)
                                norm_se = None
                                for head in range(2):
                                    pt = ps_me if head == 0 else ps_mo
                                    for sub in range(4):
                                        wl_sc = ()
                                        if head == 0 and sub == 0:
                                            wl_sc = [(ve, c_r1)]
                                        norm_se = E("scalar", lambda sc, p_=pt, h_=head, sb_=sub: sc.activation(
                                            out=otm[:, h_ * 256 + sb_ * 64: h_ * 256 + (sb_ + 1) * 64],
                                            in_=p_[:, sb_ * 65: sb_ * 65 + 64],
                                            func=AF.Copy,
                                            scale=sca[:, 8 + h_ * 4 + sb_:8 + h_ * 4 + sb_ + 1]),
                                            wait=wl_sc,
                                            inc=se if (head == 1 and sub == 3) else None)
                                c_tb = None
                                for head in range(2):
                                    tb = tb_e if head == 0 else tb_o
                                    for sub in range(4):
                                        wl_tb = ()
                                        if head == 0 and sub == 0:
                                            wl_tb = [(se, norm_se), (ve, c_r1)]
                                        c_tb = E("tensor", lambda t, tb_=tb, h_=head, sb_=sub: t.transpose(
                                            out=tb_[0:64, sb_ * 128:(sb_ + 1) * 128],
                                            in_=otm[:, h_ * 256 + sb_ * 64: h_ * 256 + (sb_ + 1) * 64],
                                            identity=id16[:]),
                                            wait=wl_tb,
                                            inc=te if (head == 1 and sub == 3) else None)
                                E("vector", lambda v, h_=hp, c_=ch: v.tensor_copy(
                                    out=Ofm[0:64, h_ * S + c_ * 512: h_ * S + (c_ + 1) * 512],
                                    in_=tb_e[0:64, 0:512]),
                                    wait=[(te, c_tb)])
                                c_vo = E("vector", lambda v, h_=hp, c_=ch: v.tensor_copy(
                                    out=Ofm[64:128, h_ * S + c_ * 512: h_ * S + (c_ + 1) * 512],
                                    in_=tb_o[0:64, 0:512]),
                                    inc=ve)
                                prev_norm = {"se": norm_se, "vofm": c_vo,
                                             "te_tt": c_tt}
                        phaseB = {"ve": ve.n, "se": se.n, "te": te.n,
                                  "vofm": prev_norm["vofm"]}

                    with ExitStack() as scx:
                        ec = scx.enter_context
                        xr = [ec(nc.sbuf_tensor(f"xr{s_}_{k}", [128, D], F32)) for k in range(2)]
                        pre_ln = ec(nc.sbuf_tensor("preln" + sfx, [128, D], F32))
                        ytile = ec(nc.sbuf_tensor("ytile" + sfx, [128, D], F32))
                        stat = ec(nc.sbuf_tensor("stat" + sfx, [128, 12], F32))
                        ps_c = [ec(nc.psum_tensor(f"ps_c{s_}_{k}", [128, 512], F32)) for k in range(4)]

                        c_resid = {}
                        c_yst = {}
                        for it in range(NT):
                            par = it % 2
                            wl_x = [(ve, c_resid.get(it - 2))]
                            if it < 2:
                                wl_x += [(ve, phaseB["ve"]), (te, phaseB["te"]),
                                         (se, phaseB["se"])]
                            E("sync", lambda s, it_=it, p_=par, sq=s_: s.dma_start(
                                out=xr[p_][:], in_=x_in[sq, it_ * 128:(it_ + 1) * 128, :]),
                                wait=wl_x, inc=ldr[par], inc_n=16)
                            nldr = ldr[par].n
                            c_mm2 = {}
                            for oc in range(2):
                                bank = 2 * par + oc
                                wl = [(g_wout, wout_ready), (ve, phaseB["vofm"]),
                                      (ve, c_resid.get(it - 2))]
                                if it < 2:
                                    wl += [(se, phaseB["se"])]
                                E("tensor", lambda t, b_=bank, o_=oc: t.matmul(
                                    out=ps_c[b_][:], lhsT=ones16[:],
                                    rhs=bout16[0:1, o_ * 512:(o_ + 1) * 512],
                                    start=True, stop=False, skip_group_check=True),
                                    wait=wl)
                                c_mmx = None
                                for pc in range(DC):
                                    c_mmx = E("tensor", lambda t, b_=bank, o_=oc, p2=pc, it_=it: t.matmul(
                                        out=ps_c[b_][:],
                                        lhsT=Ofm[:, p2 * S + it_ * 128: p2 * S + it_ * 128 + 128],
                                        rhs=w_out16[:, p2 * D + o_ * 512: p2 * D + (o_ + 1) * 512],
                                        start=False, stop=(p2 == DC - 1), skip_group_check=True),
                                        inc=te if pc == DC - 1 else None)
                                c_mm2[oc] = c_mmx
                            c_ra = None
                            for oc in range(2):
                                bank = 2 * par + oc
                                wl_r = [(te, c_mm2[oc])]
                                if oc == 0:
                                    wl_r.append((ldr[par], nldr))
                                c_ra = E("vector", lambda v, b_=bank, o_=oc, p_=par: v.tensor_tensor(
                                    out=pre_ln[:, o_ * 512:(o_ + 1) * 512],
                                    in0=ps_c[b_][:], in1=xr[p_][:, o_ * 512:(o_ + 1) * 512],
                                    op=ALU.add),
                                    wait=wl_r, inc=ve if oc == 1 else None)
                            c_resid[it] = c_ra
                            c_b0 = E("vector", lambda v: v.bn_stats(
                                out=stat[:, 0:6], in_=pre_ln[:, 0:512]),
                                wait=[(ve, c_ra)], inc=ve)
                            c_b1x = E("vector", lambda v: v.bn_stats(
                                out=stat[:, 6:12], in_=pre_ln[:, 512:1024]),
                                wait=[(ve, c_b0)], inc=ve)
                            c_ag = E("vector", lambda v: v.bn_aggr(
                                out=sca[:, 4:6], in_=stat[:, 0:12]),
                                wait=[(ve, c_b1x)], inc=ve)
                            c_t0 = E("vector", lambda v: v.tensor_scalar(
                                out=sca[:, 6:7], in0=sca[:, 5:6], scalar1=EPS,
                                scalar2=None, op0=ALU.add),
                                wait=[(ve, c_ag)], inc=ve)
                            c_sq = E("scalar", lambda sc: sc.activation(
                                out=sca[:, 6:7], in_=sca[:, 6:7], func=AF.Sqrt),
                                wait=[(ve, c_t0)], inc=se)
                            c_rc = E("vector", lambda v: v.reciprocal(
                                out=sca[:, 7:8], in_=sca[:, 6:7]),
                                wait=[(se, c_sq)], inc=ve)
                            c_y = E("vector", lambda v: v.tensor_scalar(
                                out=ytile[:], in0=pre_ln[:],
                                scalar1=sca[:, 4:5], scalar2=sca[:, 7:8],
                                op0=ALU.subtract, op1=ALU.mult),
                                wait=[(ve, c_rc), (st_y, c_yst.get(it - 1))], inc=ve)
                            c_ys = E("sync", lambda s, it_=it: s.dma_start(
                                out=y_s[it_ * 128:(it_ + 1) * 128, :], in_=ytile[:]),
                                wait=[(ve, c_y)], inc=st_y, inc_n=16)
                            c_yst[it] = c_ys
                        phaseC = {"ve": ve.n, "se": se.n, "te": te.n,
                                  "st_y": st_y.n}

            with ExitStack() as sd:
                ed = sd.enter_context
                wq1t = ed(nc.sbuf_tensor("wq1t" + sfx, [128, DC * DFF], BF16))
                wq2t = ed(nc.sbuf_tensor("wq2t" + sfx, [128, FC * D], BF16))
                ytm = ed(nc.sbuf_tensor("ytm" + sfx, [128, 4 * D], BF16))
                yfm = ed(nc.sbuf_tensor("yfm" + sfx, [128, DC * 512], BF16))
                h16 = ed(nc.sbuf_tensor("h16" + sfx, [128, FC * 512], BF16))
                yr = [ed(nc.sbuf_tensor(f"yr{s_}_{k}", [128, D], F32)) for k in range(2)]
                outt = ed(nc.sbuf_tensor("outt" + sfx, [128, D], F32))
                outf = ed(nc.sbuf_tensor("outf" + sfx, [128, D], F32))
                stat2 = ed(nc.sbuf_tensor("stat2" + sfx, [128, 12], F32))
                ps_h = [ed(nc.psum_tensor(f"ps_h{s_}_{k}", [128, 512], F32)) for k in range(2)]
                ps_f = [ed(nc.psum_tensor(f"ps_f{s_}_{k}", [128, 512], F32)) for k in range(4)]
                ps_y = [ed(nc.psum_tensor(f"ps_y{s_}_{k}", [128, D], BF16)) for k in range(2)]

                phC = [(te, phaseC["te"]), (se, phaseC["se"]), (ve, phaseC["ve"])]
                E("gpsimd", lambda g: g.dma_start(out=wq1t[:], in_=wq1t_s),
                  wait=[(st_wq, quant_done_st)] + phC, inc=g_wq, inc_n=16)
                E("gpsimd", lambda g: g.dma_start(out=wq2t[:], in_=wq2t_s),
                  inc=g_wq, inc_n=16)
                wq_ready = g_wq.n

                c_out_prev = {}
                c_yfm = {}
                c_tylast = {}
                c_res2 = {}
                c_ost = {}
                for chn in range(NCH):
                    wl_yl = [(st_y, phaseC["st_y"]), (te, c_tylast.get(chn - 1))]
                    if chn == 0:
                        wl_yl += phC
                    c_yl = E("gpsimd", lambda g, c_=chn: g.dma_start(
                        out=ytm[:].rearrange("p (t d) -> p t d", t=4),
                        in_=y_s[c_ * 512:(c_ + 1) * 512, :].rearrange("(t p) d -> p t d", p=128),
                        max_dma_last_dim=8192),
                        wait=wl_yl, inc=g_ytm, inc_n=16)
                    c_yf = None
                    c_ty = None
                    for t4 in range(4):
                        pyp = t4 % 2
                        prev_yf = c_yf
                        for dc in range(DC):
                            wl_ty = ()
                            if dc == 0:
                                wl_ty = [(g_ytm, c_yl)]
                                if t4 >= 2:
                                    wl_ty.append((ve, prev_yf))
                                if t4 == 0 and chn > 0:
                                    wl_ty.append((ve, c_yfm[chn - 1]))
                                    wl_ty.append((te, c_out_prev[chn - 1]))
                            c_ty = E("tensor", lambda t, p_=pyp, t_=t4, d_=dc: t.transpose(
                                out=ps_y[p_][:, d_ * 128:(d_ + 1) * 128],
                                in_=ytm[:, t_ * D + d_ * 128: t_ * D + (d_ + 1) * 128],
                                identity=id16[:]),
                                wait=wl_ty, inc=te if dc == DC - 1 else None)
                        c_yf = E("vector", lambda v, p_=pyp, t_=t4: v.tensor_copy(
                            out=yfm[:].rearrange("p (c n) -> p c n", c=DC)[:, :, t_ * 128:(t_ + 1) * 128],
                            in_=ps_y[p_][:].rearrange("p (c n) -> p c n", c=DC)),
                            wait=[(te, c_ty)], inc=ve)
                    c_yfm[chn] = c_yf
                    c_tylast[chn] = c_ty

                    c_relu = {}
                    for fc in range(FC):
                        hpar = fc % 2
                        c_m = None
                        for dc in range(DC):
                            wl = ()
                            if dc == 0:
                                wl = [(ve, c_yf), (g_wq, wq_ready)]
                                prev = c_relu.get(fc - 2)
                                if prev is not None:
                                    wl.append(prev)
                                if chn > 0 and fc < 2:
                                    wl.append((te, c_out_prev[chn - 1]))
                            c_m = E("tensor", lambda t, p_=hpar, f_=fc, d_=dc: t.matmul(
                                out=ps_h[p_][:],
                                lhsT=wq1t[:, d_ * DFF + f_ * 128: d_ * DFF + (f_ + 1) * 128],
                                rhs=yfm[:, d_ * 512:(d_ + 1) * 512],
                                start=(d_ == 0), stop=(d_ == DC - 1), skip_group_check=True),
                                wait=wl, inc=te if dc == DC - 1 else None)
                        if fc % 2 == 0:
                            c_r = E("vector", lambda v, p_=hpar, f_=fc: v.tensor_scalar(
                                out=h16[:, f_ * 512:(f_ + 1) * 512], in0=ps_h[p_][:],
                                scalar1=b1T[:, f_:f_ + 1], scalar2=0.0,
                                op0=ALU.add, op1=ALU.max),
                                wait=[(te, c_m)], inc=ve)
                            c_relu[fc] = (ve, c_r)
                        else:
                            c_r = E("scalar", lambda sc, p_=hpar, f_=fc: sc.activation(
                                out=h16[:, f_ * 512:(f_ + 1) * 512], in_=ps_h[p_][:],
                                func=AF.Relu, bias=b1T[:, f_:f_ + 1]),
                                wait=[(te, c_m)], inc=se)
                            c_relu[fc] = (se, c_r)
                    h_ready = [c_relu[FC - 2], c_relu[FC - 1]]

                    c_mm2l = None
                    for it4 in range(4):
                        it = chn * 4 + it4
                        par = it4 % 2
                        wl_y2 = [(st_y, phaseC["st_y"]), (ve, c_res2.get(it - 2))]
                        if it < 2:
                            wl_y2 += phC
                        E("sync", lambda s, it_=it, p_=par: s.dma_start(
                            out=yr[p_][:], in_=y_s[it_ * 128:(it_ + 1) * 128, :]),
                            wait=wl_y2, inc=ldr[par], inc_n=16)
                        nldr2 = ldr[par].n
                        c_mm2 = {}
                        for oc in range(2):
                            bank = 2 * par + oc
                            wl = list(h_ready) + [(ve, c_res2.get(it - 2))]
                            E("tensor", lambda t, b_=bank, o_=oc: t.matmul(
                                out=ps_f[b_][:], lhsT=ones16[:],
                                rhs=b2_16[0:1, o_ * 512:(o_ + 1) * 512],
                                start=True, stop=False, skip_group_check=True),
                                wait=wl)
                            c_q = None
                            for fc in range(FC):
                                c_q = E("tensor", lambda t, b_=bank, o_=oc, f_=fc, i_=it4: t.matmul(
                                    out=ps_f[b_][:],
                                    lhsT=h16[:, f_ * 512 + i_ * 128: f_ * 512 + i_ * 128 + 128],
                                    rhs=wq2t[:, f_ * D + o_ * 512: f_ * D + (o_ + 1) * 512],
                                    start=False, stop=(f_ == FC - 1), skip_group_check=True),
                                    inc=te if fc == FC - 1 else None)
                            c_mm2[oc] = c_q
                        c_mm2l = c_mm2[1]
                        c_ra = None
                        for oc in range(2):
                            bank = 2 * par + oc
                            wl_r = [(te, c_mm2[oc])]
                            if oc == 0:
                                wl_r.append((ldr[par], nldr2))
                            c_ra = E("vector", lambda v, b_=bank, o_=oc, p_=par: v.tensor_tensor(
                                out=outt[:, o_ * 512:(o_ + 1) * 512],
                                in0=ps_f[b_][:], in1=yr[p_][:, o_ * 512:(o_ + 1) * 512],
                                op=ALU.add),
                                wait=wl_r, inc=ve if oc == 1 else None)
                        c_b0 = E("vector", lambda v: v.bn_stats(
                            out=stat2[:, 0:6], in_=outt[:, 0:512]),
                            wait=[(ve, c_ra)], inc=ve)
                        c_b1x = E("vector", lambda v: v.bn_stats(
                            out=stat2[:, 6:12], in_=outt[:, 512:1024]),
                            wait=[(ve, c_b0)], inc=ve)
                        c_ag = E("vector", lambda v: v.bn_aggr(
                            out=sca[:, 4:6], in_=stat2[:, 0:12]),
                            wait=[(ve, c_b1x)], inc=ve)
                        c_t0 = E("vector", lambda v: v.tensor_scalar(
                            out=sca[:, 6:7], in0=sca[:, 5:6], scalar1=EPS,
                            scalar2=None, op0=ALU.add),
                            wait=[(ve, c_ag)], inc=ve)
                        c_sq = E("scalar", lambda sc: sc.activation(
                            out=sca[:, 6:7], in_=sca[:, 6:7], func=AF.Sqrt),
                            wait=[(ve, c_t0)], inc=se)
                        c_rc = E("vector", lambda v: v.reciprocal(
                            out=sca[:, 7:8], in_=sca[:, 6:7]),
                            wait=[(se, c_sq)], inc=ve)
                        c_o = E("vector", lambda v: v.tensor_scalar(
                            out=outf[:], in0=outt[:],
                            scalar1=sca[:, 4:5], scalar2=sca[:, 7:8],
                            op0=ALU.subtract, op1=ALU.mult),
                            wait=[(ve, c_rc), (st_out, c_ost.get(it - 1))], inc=ve)
                        c_res2[it] = c_o
                        c_ost[it] = E("sync", lambda s, it_=it, sq=s_: s.dma_start(
                            out=out[sq, it_ * 128:(it_ + 1) * 128, :], in_=outf[:]),
                            wait=[(ve, c_o)], inc=st_out, inc_n=16)
                    c_out_prev[chn] = c_mm2l
                prev_end = {"ve": ve.n, "se": se.n, "te": te.n,
                            "st_wq": quant_done_st}

        E("sync", lambda s: s.nop(), wait=[(st_out, st_out.n)])

        with nc.Block() as block:
            P.realize(block)

    return nc



def _consts():
    i = np.arange(128)[:, None]
    c = np.arange(512)[None, :]
    masks = [((128 * k + i) >= c).astype(np.float32) for k in range(4)]
    cmask = np.concatenate(masks, axis=1).astype(ml_dtypes.bfloat16)
    id32 = np.eye(128, dtype=np.float32)
    id16 = np.eye(128, dtype=np.float32).astype(ml_dtypes.bfloat16)
    return cmask, id32, id16


def make_in_map(xshard, w_qkv, b_qkv, w_out, b_out, W1, b1, W2, b2):
    cmask, id32, id16 = _consts()
    return {
        "x": np.ascontiguousarray(xshard, dtype=np.float32),
        "w_qkv": np.ascontiguousarray(w_qkv, dtype=np.float32),
        "w_out": np.ascontiguousarray(w_out, dtype=np.float32),
        "W1": np.ascontiguousarray(W1, dtype=np.float32),
        "W2": np.ascontiguousarray(W2, dtype=np.float32),
        "bqkv_blk": np.ascontiguousarray(
            np.asarray(b_qkv, dtype=np.float32).reshape(24, 128)),
        "bqkv_row": np.ascontiguousarray(
            np.asarray(b_qkv, dtype=np.float32).reshape(1, 3 * D)),
        "b1_blk": np.ascontiguousarray(
            np.asarray(b1, dtype=np.float32).reshape(32, 128)),
        "bout_row": np.ascontiguousarray(
            np.asarray(b_out, dtype=np.float32).reshape(1, D)),
        "b2_row": np.ascontiguousarray(
            np.asarray(b2, dtype=np.float32).reshape(1, D)),
        "cmask": cmask, "id32": id32, "id16": id16,
    }


def kernel(L0_attn_x, L0_attn_w_qkv, L0_attn_b_qkv, L0_attn_w_out,
           L0_attn_b_out, L0_ffn_W1, L0_ffn_b1, L0_ffn_W2, L0_ffn_b2):
    x = np.ascontiguousarray(np.asarray(L0_attn_x, dtype=np.float32))
    B = x.shape[0]
    per = B // NCORES
    nc = build_core_program(per)
    in_maps = []
    for c in range(NCORES):
        in_maps.append(make_in_map(
            x[c * per:(c + 1) * per], L0_attn_w_qkv, L0_attn_b_qkv,
            L0_attn_w_out, L0_attn_b_out, L0_ffn_W1, L0_ffn_b1,
            L0_ffn_W2, L0_ffn_b2))
    res = run_bass_kernel_spmd(nc, in_maps, core_ids=list(range(NCORES)))
    return np.concatenate([np.asarray(r["out"]) for r in res.results], axis=0)


# version 04
# speedup vs baseline: 1.3477x; 1.3477x over previous
import numpy as np
import ml_dtypes
from contextlib import ExitStack

import concourse.bass as bass
import concourse.mybir as mybir
from concourse.bass_utils import run_bass_kernel_spmd

F32 = mybir.dt.float32
BF16 = mybir.dt.bfloat16
AF = mybir.ActivationFunctionType
ALU = mybir.AluOpType
AX = mybir.AxisListType

NCORES = 8
S = 2048
D = 1024
H = 16
HD = 64
HPAIRS = 8
DFF = 4096
NT = S // 128
NCH = S // 512
DC = D // 128
FC = DFF // 128
EPS = 1e-5
SCALE = 0.125

ENGINES = ("tensor", "vector", "scalar", "gpsimd", "sync")


class Sem:
    def __init__(self, nc, stack, name):
        self.h = stack.enter_context(nc.semaphore(name))
        self.n = 0
        self.name = name


class Prog:
    def __init__(self):
        self.ops = {e: [] for e in ENGINES}

    def emit(self, eng, fn, wait=(), inc=None, inc_n=1):
        cnt = None
        if inc is not None:
            inc.n += inc_n
            cnt = inc.n
        w = tuple((s, c) for (s, c) in wait if s is not None and c)
        self.ops[eng].append((w, fn, inc, inc_n))
        return cnt

    def realize(self, block):
        for eng in ENGINES:
            ops = self.ops[eng]
            if not ops:
                continue

            def body(E, _ops=ops):
                for w, fn, inc, inc_n in _ops:
                    for (sem, cnt) in w:
                        E.wait_ge(sem.h, cnt)
                    instr = fn(E)
                    if inc is not None:
                        instr.then_inc(inc.h, inc_n)

            getattr(block, eng)(body)


def build_core_program(nseq=4):
    nc = bass.Bass("TRN2", target_bir_lowering=False)

    x_in = nc.dram_tensor("x", [nseq, S, D], F32, kind="ExternalInput").ap()
    w_qkv = nc.dram_tensor("w_qkv", [D, 3 * D], F32, kind="ExternalInput").ap()
    w_out = nc.dram_tensor("w_out", [D, D], F32, kind="ExternalInput").ap()
    W1 = nc.dram_tensor("W1", [DFF, D], F32, kind="ExternalInput").ap()
    W2 = nc.dram_tensor("W2", [D, DFF], F32, kind="ExternalInput").ap()
    bqkv_blk = nc.dram_tensor("bqkv_blk", [24, 128], F32, kind="ExternalInput").ap()
    bqkv_row = nc.dram_tensor("bqkv_row", [1, 3 * D], F32, kind="ExternalInput").ap()
    b1_blk = nc.dram_tensor("b1_blk", [32, 128], F32, kind="ExternalInput").ap()
    bout_row = nc.dram_tensor("bout_row", [1, D], F32, kind="ExternalInput").ap()
    b2_row = nc.dram_tensor("b2_row", [1, D], F32, kind="ExternalInput").ap()
    cmask_in = nc.dram_tensor("cmask", [128, 4 * 1024], BF16, kind="ExternalInput").ap()
    id32_in = nc.dram_tensor("id32", [128, 128], F32, kind="ExternalInput").ap()
    id16_in = nc.dram_tensor("id16", [128, 128], BF16, kind="ExternalInput").ap()

    wq1t_s = nc.dram_tensor("wq1t_s", [128, DC * DFF], BF16, kind="Internal").ap()
    wq2t_s = nc.dram_tensor("wq2t_s", [128, FC * D], BF16, kind="Internal").ap()
    y_s = nc.dram_tensor("y_s", [S, D], F32, kind="Internal").ap()

    out = nc.dram_tensor("out", [nseq, S, D], F32, kind="ExternalOutput").ap()

    P = Prog()

    with ExitStack() as top:
        et = top.enter_context

        cmask = et(nc.sbuf_tensor("cmask_sb", [128, 4 * 1024], BF16))
        id32 = et(nc.sbuf_tensor("id32_sb", [128, 128], F32))
        id16 = et(nc.sbuf_tensor("id16_sb", [128, 128], BF16))
        ones16 = et(nc.sbuf_tensor("ones16", [1, 128], BF16))
        onesc32 = et(nc.sbuf_tensor("onesc32", [128, 1], F32))
        ones32 = et(nc.sbuf_tensor("ones32", [1, 128], F32))
        sca = et(nc.sbuf_tensor("sca", [128, 40], F32))

        sems = {}
        for name in ("te", "ve", "se", "ldc", "ldx0", "ldx1", "lW", "st_wq",
                     "g_wqkv", "g_wout", "g_wq", "g_ytm", "ldr0", "ldr1",
                     "st_y", "st_out"):
            sems[name] = Sem(nc, top, name)
        te, ve, se = sems["te"], sems["ve"], sems["se"]
        ldc = sems["ldc"]
        ldx = (sems["ldx0"], sems["ldx1"])
        lW = sems["lW"]
        st_wq = sems["st_wq"]
        g_wqkv, g_wout, g_wq, g_ytm = (sems["g_wqkv"], sems["g_wout"],
                                       sems["g_wq"], sems["g_ytm"])
        ldr = (sems["ldr0"], sems["ldr1"])
        st_y, st_out = sems["st_y"], sems["st_out"]

        E = P.emit

        if True:
            for src, dst in ((cmask_in, cmask), (id32_in, id32), (id16_in, id16)):
                E("sync", lambda s, a=src, b=dst: s.dma_start(out=b[:], in_=a),
                  inc=ldc, inc_n=16)
            c_all = ldc.n
            E("vector", lambda v: v.memset(ones16[:], 1.0))
            E("vector", lambda v: v.memset(onesc32[:], 1.0))
            c_ones = E("vector", lambda v: v.memset(ones32[:], 1.0),
                       wait=[(ldc, c_all)], inc=ve)

        if True:
            def quantize(widx, W_hbm, nchunks_in, in_chunk, out_cols, scratch,
                         thr_col, nthr_col, extra_wait):
                with ExitStack() as qs:
                    eq = qs.enter_context
                    wf = eq(nc.sbuf_tensor(f"wf{widx}", [128, nchunks_in * in_chunk], F32))
                    stg = eq(nc.sbuf_tensor(f"stg{widx}", [128, out_cols], BF16))
                    red = eq(nc.sbuf_tensor(f"red{widx}", [128, 256], F32))
                    nbuf = eq(nc.sbuf_tensor(f"nbuf{widx}", [128, 128], F32))
                    ps_q = eq(nc.psum_tensor(f"ps_q{widx}", [128, 128], F32))
                    ps_tr = [eq(nc.psum_tensor(f"ps_tq{k}_{widx}", [128, 128], F32))
                             for k in range(2)]

                    c_w = E("sync", lambda s: s.dma_start(
                        out=wf[:].rearrange("p (c n) -> p c n", c=nchunks_in),
                        in_=W_hbm.rearrange("(c p) n -> p c n", p=128)),
                        wait=[(st_wq, st_wq.n)] + list(extra_wait),
                        inc=lW, inc_n=16)

                    nel = 128 * nchunks_in * in_chunk
                    c_r1 = E("vector", lambda v: v.tensor_reduce(
                        out=red[:], in_=wf[:].rearrange("p (a b) -> p a b", a=256),
                        axis=AX.X, op=ALU.add, apply_absolute_value=True),
                        wait=[(lW, c_w)], inc=ve)
                    c_r2 = E("vector", lambda v: v.tensor_reduce(
                        out=red[:, 0:1], in_=red[:], axis=AX.X, op=ALU.add,
                        apply_absolute_value=True),
                        wait=[(ve, c_r1)], inc=ve)
                    c_m1 = E("tensor", lambda t: t.matmul(
                        out=ps_q[0:1, 0:1], lhsT=onesc32[:], rhs=red[:, 0:1],
                        start=True, stop=True, skip_group_check=True),
                        wait=[(ve, c_r2)], inc=te)
                    c_g = E("vector", lambda v: v.tensor_copy(
                        out=sca[0:1, 20:21], in_=ps_q[0:1, 0:1]),
                        wait=[(te, c_m1)], inc=ve)
                    c_m2 = E("tensor", lambda t: t.matmul(
                        out=ps_q[:, 0:1], lhsT=ones32[:], rhs=sca[0:1, 20:21],
                        start=True, stop=True, skip_group_check=True),
                        wait=[(ve, c_g)], inc=te)
                    c_thr = E("vector", lambda v: v.tensor_scalar(
                        out=sca[:, thr_col:thr_col + 1], in0=ps_q[:, 0:1],
                        scalar1=0.5 / nel, scalar2=0.5 * EPS, op0=ALU.mult, op1=ALU.add),
                        wait=[(te, c_m2)], inc=ve)
                    c_nthr = E("vector", lambda v: v.tensor_scalar(
                        out=sca[:, nthr_col:nthr_col + 1], in0=sca[:, thr_col:thr_col + 1],
                        scalar1=-1.0, scalar2=None, op0=ALU.mult),
                        wait=[(ve, c_thr)], inc=ve)

                    nblk_out = DC if widx == 1 else FC
                    nblk_in = FC if widx == 1 else DC
                    last_stt = c_nthr
                    prev_stt = {}
                    for bi in range(nblk_in):
                        for bo in range(nblk_out):
                            par = (bi * nblk_out + bo) % 2
                            src = wf[:, bi * in_chunk + bo * 128: bi * in_chunk + bo * 128 + 128]
                            c_tp = E("tensor", lambda t, s_=src, p_=ps_tr[par]: t.transpose(
                                out=p_[:], in_=s_, identity=id32[:]),
                                wait=[(lW, c_w), (ve, c_nthr),
                                      (ve, prev_stt.get(par))], inc=te)
                            c_n = E("vector", lambda v, p_=ps_tr[par]: v.tensor_scalar(
                                out=nbuf[:], in0=p_[:],
                                scalar1=sca[:, nthr_col:nthr_col + 1], scalar2=None,
                                op0=ALU.is_lt),
                                wait=[(te, c_tp), (ve, last_stt)], inc=ve)
                            if widx == 1:
                                dst = stg[:, bo * DFF + bi * 128: bo * DFF + bi * 128 + 128]
                            else:
                                dst = stg[:, bo * D + bi * 128: bo * D + bi * 128 + 128]
                            c_stt = E("vector", lambda v, p_=ps_tr[par], d_=dst: v.scalar_tensor_tensor(
                                out=d_, in0=p_[:], scalar=sca[:, thr_col:thr_col + 1],
                                in1=nbuf[:], op0=ALU.is_gt, op1=ALU.subtract),
                                wait=[(ve, c_n)], inc=ve)
                            prev_stt[par] = c_stt
                            last_stt = c_stt
                    E("sync", lambda s: s.dma_start(out=scratch, in_=stg[:]),
                      wait=[(ve, last_stt)], inc=st_wq, inc_n=16)

            quantize(1, W1, FC, D, DC * DFF, wq1t_s, 0, 1, [])
            quantize(2, W2, DC, DFF, FC * D, wq2t_s, 2, 3, [])
            quant_done_st = st_wq.n

        bqkvT = et(nc.sbuf_tensor("bqkvT", [128, 24], F32))
        b1T = et(nc.sbuf_tensor("b1T", [128, 32], F32))
        bv16 = et(nc.sbuf_tensor("bv16", [1, D], BF16))
        bout16 = et(nc.sbuf_tensor("bout16", [1, D], BF16))
        b2_16 = et(nc.sbuf_tensor("b2_16", [1, D], BF16))
        with ExitStack() as ist:
            ei = ist.enter_context
            btmp = ei(nc.sbuf_tensor("btmp", [32, 128], F32))
            brow = ei(nc.sbuf_tensor("brow", [1, 3 * D], F32))
            ps_i = ei(nc.psum_tensor("ps_i", [128, 64], F32))
            qve = ve.n
            E("sync", lambda s: s.dma_start(out=brow[:], in_=bqkv_row),
              wait=[(st_wq, quant_done_st), (ve, qve)], inc=ldc, inc_n=16)
            c_bv = E("vector", lambda v: v.tensor_copy(out=bv16[:], in_=brow[0:1, 2 * D:3 * D]),
                     wait=[(ldc, ldc.n)], inc=ve)
            E("sync", lambda s: s.dma_start(out=btmp[0:24, :], in_=bqkv_blk),
              wait=[(ve, c_bv)], inc=ldc, inc_n=16)
            c_b1 = ldc.n
            c_t1 = E("tensor", lambda t: t.transpose(out=ps_i[:, 0:24], in_=btmp[0:24, :], identity=id32[0:24, 0:24]),
                     wait=[(ldc, c_b1), (ve, qve)], inc=te)
            c_e1 = E("vector", lambda v: v.tensor_copy(out=bqkvT[:], in_=ps_i[:, 0:24]),
                     wait=[(te, c_t1)], inc=ve)
            E("sync", lambda s: s.dma_start(out=btmp[0:32, :], in_=b1_blk),
              wait=[(ve, c_e1)], inc=ldc, inc_n=16)
            c_b2 = ldc.n
            c_t2 = E("tensor", lambda t: t.transpose(out=ps_i[:, 32:64], in_=btmp[0:32, :], identity=id32[0:32, 0:32]),
                     wait=[(ldc, c_b2)], inc=te)
            c_e2 = E("vector", lambda v: v.tensor_copy(out=b1T[:], in_=ps_i[:, 32:64]),
                     wait=[(te, c_t2)], inc=ve)
            E("sync", lambda s: s.dma_start(out=brow[0:1, 0:D], in_=bout_row),
              wait=[(ve, c_e2)], inc=ldc, inc_n=16)
            c_bo = E("vector", lambda v: v.tensor_copy(out=bout16[:], in_=brow[0:1, 0:D]),
                     wait=[(ldc, ldc.n)], inc=ve)
            E("sync", lambda s: s.dma_start(out=brow[0:1, 0:D], in_=b2_row),
              wait=[(ve, c_bo)], inc=ldc, inc_n=16)
            c_binit = E("vector", lambda v: v.tensor_copy(out=b2_16[:], in_=brow[0:1, 0:D]),
                        wait=[(ldc, ldc.n)], inc=ve)

        prev_end = {"ve": ve.n, "se": se.n, "te": te.n, "st_wq": quant_done_st}

        for s_ in range(nseq):
            sfx = f"_{s_}"
            with ExitStack() as sab:
                ea = sab.enter_context
                Qf = ea(nc.sbuf_tensor("Qf" + sfx, [128, DC * S], BF16))
                Kf = ea(nc.sbuf_tensor("Kf" + sfx, [128, DC * S], BF16))
                Vaug = ea(nc.sbuf_tensor("Vaug" + sfx, [128, NT * 1040], BF16))

                with ExitStack() as sa:
                    e2 = sa.enter_context
                    xfm = e2(nc.sbuf_tensor("xfm" + sfx, [128, DC * S], BF16))
                    wq16 = e2(nc.sbuf_tensor("wq16" + sfx, [128, DC * 3 * D], BF16))
                    xin = [e2(nc.sbuf_tensor(f"xin{s_}_{k}", [128, D], F32)) for k in range(2)]
                    ps_xt = [e2(nc.psum_tensor(f"ps_xt{s_}_{k}", [128, D], F32)) for k in range(2)]
                    ps_qk = [e2(nc.psum_tensor(f"ps_qk{s_}_{k}", [128, 512], F32)) for k in range(2)]
                    ps_v = [e2(nc.psum_tensor(f"ps_v{s_}_{k}", [128, 512], F32)) for k in range(2)]

                    for dcw in range(DC):
                        E("gpsimd", lambda g, d_=dcw: g.dma_start(
                            out=wq16[:, d_ * 3 * D:(d_ + 1) * 3 * D],
                            in_=w_qkv[d_ * 128:(d_ + 1) * 128, :],
                            max_dma_last_dim=8192),
                            wait=([(ve, prev_end["ve"]), (se, prev_end["se"]),
                                   (te, prev_end["te"]), (st_wq, prev_end["st_wq"])]
                                  if dcw == 0 else ()),
                            inc=g_wqkv, inc_n=16)
                    wqkv_ready = g_wqkv.n

                    v_ready = []
                    xfm_ready = []
                    c_evac = {}
                    c_trans = {}
                    c_vevac = {}
                    for it in range(NT):
                        par = it % 2
                        wl_load = [(te, c_trans.get(it - 2))]
                        if it < 2:
                            wl_load += [(ve, prev_end["ve"]), (se, prev_end["se"]),
                                        (te, prev_end["te"]),
                                        (st_wq, prev_end["st_wq"])]
                        E("sync", lambda s, it_=it, p_=par, sq=s_: s.dma_start(
                            out=xin[p_][:], in_=x_in[sq, it_ * 128:(it_ + 1) * 128, :]),
                            wait=wl_load, inc=ldx[par], inc_n=16)
                        nld = ldx[par].n
                        c_t = None
                        for dc in range(DC):
                            wl_t = []
                            if dc == 0:
                                wl_t = [(ldx[par], nld), (ve, c_evac.get(it - 2))]
                                if it < 2:
                                    wl_t.append((ve, prev_end["ve"]))
                            c_t = E("tensor", lambda t, p_=par, d_=dc: t.transpose(
                                out=ps_xt[p_][:, d_ * 128:(d_ + 1) * 128],
                                in_=xin[p_][:, d_ * 128:(d_ + 1) * 128],
                                identity=id32[:]),
                                wait=wl_t, inc=te if dc == DC - 1 else None)
                        c_trans[it] = c_t
                        c_e = E("vector", lambda v, p_=par, it_=it: v.tensor_copy(
                            out=xfm[:].rearrange("p (c n) -> p c n", c=DC)[:, :, it_ * 128:(it_ + 1) * 128],
                            in_=ps_xt[p_][:].rearrange("p (c n) -> p c n", c=DC)),
                            wait=[(te, c_trans[it])], inc=ve)
                        c_evac[it] = c_e
                        xfm_ready.append(c_e)
                        E("vector", lambda v, it_=it: v.memset(
                            Vaug[:, it_ * 1040:(it_ + 1) * 1040].rearrange(
                                "p (g c) -> p g c", c=65)[:, :, 64:65], 1.0))
                        for vc in range(2):
                            vpar = (it * 2 + vc) % 2
                            E("tensor", lambda t, p_=vpar, v_=vc: t.matmul(
                                out=ps_v[p_][:], lhsT=ones16[:],
                                rhs=bv16[0:1, v_ * 512:(v_ + 1) * 512],
                                start=True, stop=False, skip_group_check=True),
                                wait=[(ve, c_e), (g_wqkv, wqkv_ready),
                                      (ve, c_vevac.get((it * 2 + vc) - 2))])
                            c_mm = None
                            for dc in range(DC):
                                c_mm = E("tensor", lambda t, p_=vpar, v_=vc, d_=dc, it_=it: t.matmul(
                                    out=ps_v[p_][:],
                                    lhsT=xfm[:, d_ * S + it_ * 128: d_ * S + it_ * 128 + 128],
                                    rhs=wq16[:, d_ * 3 * D + 2 * D + v_ * 512: d_ * 3 * D + 2 * D + (v_ + 1) * 512],
                                    start=False, stop=(d_ == DC - 1), skip_group_check=True),
                                    inc=te if dc == DC - 1 else None)
                            c_vv = E("vector", lambda v, p_=vpar, v2=vc, it_=it: v.tensor_copy(
                                out=Vaug[:, it_ * 1040 + v2 * 8 * 65: it_ * 1040 + (v2 * 8 + 8) * 65].rearrange(
                                    "p (g c) -> p g c", c=65)[:, :, 0:64],
                                in_=ps_v[p_][:].rearrange("p (g c) -> p g c", c=64)),
                                wait=[(te, c_mm)], inc=ve)
                            c_vevac[it * 2 + vc] = c_vv
                        v_ready.append(c_vevac[it * 2 + 1])

                    qk_ready = []
                    c_qkevac = {}
                    for ch in range(NCH):
                        for qc in range(16):
                            qpar = (ch * 16 + qc) % 2
                            c_mm = None
                            for dc in range(DC):
                                wl_q = ()
                                if dc == 0:
                                    wl_q = [(ve, xfm_ready[4 * ch + 3]),
                                            (g_wqkv, wqkv_ready),
                                            (se, c_qkevac.get((ch * 16 + qc) - 2))]
                                c_mm = E("tensor", lambda t, p_=qpar, q_=qc, d_=dc, c_=ch: t.matmul(
                                    out=ps_qk[p_][:],
                                    lhsT=wq16[:, d_ * 3 * D + q_ * 128: d_ * 3 * D + (q_ + 1) * 128],
                                    rhs=xfm[:, d_ * S + c_ * 512: d_ * S + (c_ + 1) * 512],
                                    start=(d_ == 0), stop=(d_ == DC - 1), skip_group_check=True),
                                    wait=wl_q, inc=te if dc == DC - 1 else None)
                            dstbuf = Qf if qc < 8 else Kf
                            dcol = (qc % 8) * S + ch * 512
                            c_sev = E("scalar", lambda sc, p_=qpar, b_=dstbuf, col=dcol, q_=qc: sc.activation(
                                out=b_[:, col:col + 512], in_=ps_qk[p_][:],
                                func=AF.Identity, bias=bqkvT[:, q_:q_ + 1]),
                                wait=[(te, c_mm)], inc=se)
                            c_qkevac[ch * 16 + qc] = c_sev
                        qk_ready.append(c_qkevac[ch * 16 + 15])
                    phaseA = {"ve": ve.n, "se": se.n, "te": te.n}

                with ExitStack() as sb:
                    eb = sb.enter_context
                    Ofm = eb(nc.sbuf_tensor("Ofm" + sfx, [128, HPAIRS * S], BF16))
                    w_out16 = eb(nc.sbuf_tensor("wout16" + sfx, [128, DC * D], BF16))

                    E("gpsimd", lambda g: g.dma_start(
                        out=w_out16[:].rearrange("p (c n) -> p c n", c=DC),
                        in_=w_out.rearrange("(c p) n -> p c n", p=128)),
                        wait=[(se, phaseA["se"]), (te, phaseA["te"]),
                              (ve, phaseA["ve"])],
                        inc=g_wout, inc_n=16)
                    wout_ready = g_wout.n

                    with ExitStack() as sbp:
                        ep = sbp.enter_context
                        Pb = [ep(nc.sbuf_tensor(f"Pb{s_}_{k}", [128, 1024], BF16)) for k in range(2)]
                        Pd = [ep(nc.sbuf_tensor(f"Pd{s_}_{k}", [128, 1024], BF16)) for k in range(2)]
                        of32 = ep(nc.sbuf_tensor("of32" + sfx, [128, 1024], F32))
                        otm = ep(nc.sbuf_tensor("otm" + sfx, [128, 512], BF16))
                        ps_s2 = [ep(nc.psum_tensor(f"ps_s{s_}_{k}", [128, 1024], F32)) for k in range(2)]
                        ps_o = ep(nc.psum_tensor("ps_o" + sfx, [128, 1024], F32))
                        ps_me = ep(nc.psum_tensor("ps_me" + sfx, [128, 512], F32))
                        ps_mo = ep(nc.psum_tensor("ps_mo" + sfx, [128, 512], F32))
                        tb_e = ps_me[:].bitcast(BF16)
                        tb_o = ps_mo[:].bitcast(BF16)

                        prev = {"c_c1": None, "c_tt8": None, "c_r1": None,
                                "norm_se": None, "c_tb8": None, "vofm": None,
                                "last_exp": None, "last_pv": None}
                        tail = [None]

                        def emit_S(hp, ch, jt, jdx, c_exp, first_wl):
                            sl = jdx % 2
                            wl = list(first_wl)
                            if jdx >= 2:
                                wl.append((se, c_exp[jt - 2]))
                            E("tensor", lambda t, h_=hp, c_=ch, j_=jt, s0=sl: t.matmul(
                                out=ps_s2[s0][:, 0:512],
                                lhsT=Kf[0:64, h_ * S + j_ * 128: h_ * S + (j_ + 1) * 128],
                                rhs=Qf[0:64, h_ * S + c_ * 512: h_ * S + (c_ + 1) * 512],
                                start=True, stop=True, skip_group_check=True,
                                tile_position=(0, 0)),
                                wait=wl)
                            return E("tensor", lambda t, h_=hp, c_=ch, j_=jt, s0=sl: t.matmul(
                                out=ps_s2[s0][:, 512:1024],
                                lhsT=Kf[64:128, h_ * S + j_ * 128: h_ * S + (j_ + 1) * 128],
                                rhs=Qf[64:128, h_ * S + c_ * 512: h_ * S + (c_ + 1) * 512],
                                start=True, stop=True, skip_group_check=True,
                                tile_position=(64, 0)),
                                inc=te)

                        for hp in range(HPAIRS):
                            for ch in range(NCH):
                                jt0 = 4 * ch
                                n = NT - jt0
                                c_s = {}
                                c_exp = {}
                                c_pvl = {}
                                c_mk = {}
                                base_wl = []
                                if prev["last_exp"] is not None:
                                    base_wl.append((se, prev["last_exp"]))
                                else:
                                    base_wl += [(te, phaseA["te"]), (ve, phaseA["ve"]),
                                                (se, phaseA["se"])]
                                base_wl.append((se, qk_ready[ch]))
                                c_s[jt0] = emit_S(hp, ch, jt0, 0, c_exp,
                                                  base_wl + [(ve, v_ready[jt0])])
                                if n > 1:
                                    c_s[jt0 + 1] = emit_S(hp, ch, jt0 + 1, 1, c_exp,
                                                          base_wl + [(ve, v_ready[jt0 + 1])])
                                for jdx in range(n):
                                    jt = jt0 + jdx
                                    sl = jdx % 2
                                    wl2 = [(te, c_s[jt])]
                                    if jdx >= 2:
                                        wl2.append((te, c_pvl[jt - 2]))
                                    elif prev["last_pv"] is not None:
                                        wl2.append((te, prev["last_pv"]))
                                    c_e2 = E("scalar", lambda sc, s0=sl: sc.activation(
                                        out=Pb[s0][:], in_=ps_s2[s0][:],
                                        func=AF.Exp, scale=SCALE),
                                        wait=wl2, inc=se)
                                    c_exp[jt] = c_e2
                                    if jdx < 4:
                                        wl_m = [(se, c_e2)]
                                        pdw = c_pvl.get(jt - 2, prev["last_pv"])
                                        if pdw is not None:
                                            wl_m.append((te, pdw))
                                        c_m = E("vector", lambda v, s0=sl, k_=jdx, pi=jdx % 2: v.tensor_tensor(
                                            out=Pd[pi][:], in0=Pb[s0][:],
                                            in1=cmask[:, k_ * 1024:(k_ + 1) * 1024],
                                            op=ALU.mult),
                                            wait=wl_m, inc=ve)
                                        c_mk[jt] = c_m
                                        src = Pd[jdx % 2]
                                        wl3 = [(ve, c_m)]
                                    else:
                                        src = Pb[sl]
                                        wl3 = [(se, c_e2)]
                                    if jdx == 0:
                                        if prev["c_c1"] is not None:
                                            wl3.append((ve, prev["c_c1"]))
                                        else:
                                            wl3.append((ve, phaseA["ve"]))
                                    E("tensor", lambda t, h_=hp, j_=jt, s2=src, f_=(jdx == 0), l_=(jdx == n - 1): t.matmul(
                                        out=ps_o[0:65, 0:512],
                                        lhsT=Vaug[:, j_ * 1040 + (2 * h_) * 65: j_ * 1040 + (2 * h_ + 1) * 65],
                                        rhs=s2[:, 0:512], start=f_, stop=l_,
                                        skip_group_check=True),
                                        wait=wl3)
                                    c_p = E("tensor", lambda t, h_=hp, j_=jt, s2=src, f_=(jdx == 0), l_=(jdx == n - 1): t.matmul(
                                        out=ps_o[0:65, 512:1024],
                                        lhsT=Vaug[:, j_ * 1040 + (2 * h_ + 1) * 65: j_ * 1040 + (2 * h_ + 2) * 65],
                                        rhs=s2[:, 512:1024], start=f_, stop=l_,
                                        skip_group_check=True),
                                        inc=te)
                                    c_pvl[jt] = c_p
                                    if jdx + 2 < n:
                                        c_s[jt + 2] = emit_S(hp, ch, jt + 2, jdx + 2, c_exp,
                                                             [(ve, v_ready[jt + 2])])
                                    if jdx == 0 and tail[0] is not None:
                                        tail[0]()
                                        tail[0] = None
                                last_pv = c_pvl[jt0 + n - 1]
                                last_exp = c_exp[jt0 + n - 1]

                                wl_c = [(te, last_pv)]
                                if prev["c_tt8"] is not None:
                                    wl_c.append((te, prev["c_tt8"]))
                                c_c1 = E("vector", lambda v: v.tensor_copy(
                                    out=of32[0:65, :], in_=ps_o[0:65, :]),
                                    wait=wl_c, inc=ve)
                                c_tt = None
                                for head in range(2):
                                    pt = ps_me if head == 0 else ps_mo
                                    for sub in range(4):
                                        wl_tt = ()
                                        if head == 0 and sub == 0:
                                            wl_tt = [(ve, c_c1)]
                                            if prev["norm_se"] is not None:
                                                wl_tt += [(se, prev["norm_se"]),
                                                          (ve, prev["c_r1"])]
                                        c_tt = E("tensor", lambda t, p_=pt, h_=head, sb_=sub: t.transpose(
                                            out=p_[:, sb_ * 65:(sb_ + 1) * 65],
                                            in_=of32[0:65, h_ * 512 + sb_ * 128: h_ * 512 + (sb_ + 1) * 128],
                                            identity=id32[0:65, 0:65]),
                                            wait=wl_tt,
                                            inc=te if (head == 1 and sub == 3) else None)
                                c_r0 = E("vector", lambda v: v.reciprocal(
                                    out=sca[:, 8:12],
                                    in_=ps_me[:, 0:260].rearrange("p (s c) -> p s c", c=65)[:, :, 64:65]),
                                    wait=[(te, c_tt)], inc=ve)
                                c_r1 = E("vector", lambda v: v.reciprocal(
                                    out=sca[:, 12:16],
                                    in_=ps_mo[:, 0:260].rearrange("p (s c) -> p s c", c=65)[:, :, 64:65]),
                                    wait=[(ve, c_r0)], inc=ve)

                                def make_tail(hp_=hp, ch_=ch, c_r1_=c_r1,
                                              c_tb8_prev=prev["c_tb8"]):
                                    def tail_fn():
                                        norm_se = None
                                        for head in range(2):
                                            pt = ps_me if head == 0 else ps_mo
                                            for sub in range(4):
                                                wl_sc = ()
                                                if head == 0 and sub == 0:
                                                    wl_sc = [(ve, c_r1_)]
                                                norm_se = E("scalar", lambda sc, p_=pt, h_=head, sb_=sub: sc.activation(
                                                    out=otm[:, h_ * 256 + sb_ * 64: h_ * 256 + (sb_ + 1) * 64],
                                                    in_=p_[:, sb_ * 65: sb_ * 65 + 64],
                                                    func=AF.Copy,
                                                    scale=sca[:, 8 + h_ * 4 + sb_:8 + h_ * 4 + sb_ + 1]),
                                                    wait=wl_sc,
                                                    inc=se if (head == 1 and sub == 3) else None)
                                        c_tb = None
                                        for head in range(2):
                                            tb = tb_e if head == 0 else tb_o
                                            for sub in range(4):
                                                wl_tb = ()
                                                if head == 0 and sub == 0:
                                                    wl_tb = [(se, norm_se), (ve, c_r1_)]
                                                c_tb = E("tensor", lambda t, tb_=tb, h_=head, sb_=sub: t.transpose(
                                                    out=tb_[0:64, sb_ * 128:(sb_ + 1) * 128],
                                                    in_=otm[:, h_ * 256 + sb_ * 64: h_ * 256 + (sb_ + 1) * 64],
                                                    identity=id16[:]),
                                                    wait=wl_tb,
                                                    inc=te if (head == 1 and sub == 3) else None)
                                        E("vector", lambda v, h_=hp_, c_=ch_: v.tensor_copy(
                                            out=Ofm[0:64, h_ * S + c_ * 512: h_ * S + (c_ + 1) * 512],
                                            in_=tb_e[0:64, 0:512]),
                                            wait=[(te, c_tb)])
                                        c_vo = E("vector", lambda v, h_=hp_, c_=ch_: v.tensor_copy(
                                            out=Ofm[64:128, h_ * S + c_ * 512: h_ * S + (c_ + 1) * 512],
                                            in_=tb_o[0:64, 0:512]),
                                            inc=ve)
                                        prev["norm_se"] = norm_se
                                        prev["c_tb8"] = c_tb
                                        prev["vofm"] = c_vo
                                    return tail_fn

                                tail[0] = make_tail()
                                prev["c_c1"] = c_c1
                                prev["c_tt8"] = c_tt
                                prev["c_r1"] = c_r1
                                prev["last_exp"] = last_exp
                                prev["last_pv"] = last_pv
                        tail[0]()
                        tail[0] = None
                        phaseB = {"ve": ve.n, "se": se.n, "te": te.n,
                                  "vofm": prev["vofm"]}


                    with ExitStack() as scx:
                        ec = scx.enter_context
                        xr = [ec(nc.sbuf_tensor(f"xr{s_}_{k}", [128, D], F32)) for k in range(2)]
                        pre_ln = ec(nc.sbuf_tensor("preln" + sfx, [128, D], F32))
                        ytile = ec(nc.sbuf_tensor("ytile" + sfx, [128, D], F32))
                        ps_c = [ec(nc.psum_tensor(f"ps_c{s_}_{k}", [128, 512], F32)) for k in range(4)]

                        c_resid = {}
                        c_yst = {}
                        for it in range(NT):
                            par = it % 2
                            wl_x = [(ve, c_resid.get(it - 2))]
                            if it < 2:
                                wl_x += [(ve, phaseB["ve"]), (te, phaseB["te"]),
                                         (se, phaseB["se"])]
                            E("sync", lambda s, it_=it, p_=par, sq=s_: s.dma_start(
                                out=xr[p_][:], in_=x_in[sq, it_ * 128:(it_ + 1) * 128, :]),
                                wait=wl_x, inc=ldr[par], inc_n=16)
                            nldr = ldr[par].n
                            c_mm2 = {}
                            for oc in range(2):
                                bank = 2 * par + oc
                                wl = [(g_wout, wout_ready), (ve, phaseB["vofm"]),
                                      (ve, c_resid.get(it - 2))]
                                if it < 2:
                                    wl += [(se, phaseB["se"])]
                                E("tensor", lambda t, b_=bank, o_=oc: t.matmul(
                                    out=ps_c[b_][:], lhsT=ones16[:],
                                    rhs=bout16[0:1, o_ * 512:(o_ + 1) * 512],
                                    start=True, stop=False, skip_group_check=True),
                                    wait=wl)
                                c_mmx = None
                                for pc in range(DC):
                                    c_mmx = E("tensor", lambda t, b_=bank, o_=oc, p2=pc, it_=it: t.matmul(
                                        out=ps_c[b_][:],
                                        lhsT=Ofm[:, p2 * S + it_ * 128: p2 * S + it_ * 128 + 128],
                                        rhs=w_out16[:, p2 * D + o_ * 512: p2 * D + (o_ + 1) * 512],
                                        start=False, stop=(p2 == DC - 1), skip_group_check=True),
                                        inc=te if pc == DC - 1 else None)
                                c_mm2[oc] = c_mmx
                            c_ra = None
                            for oc in range(2):
                                bank = 2 * par + oc
                                wl_r = [(te, c_mm2[oc])]
                                if oc == 0:
                                    wl_r.append((ldr[par], nldr))
                                c_ra = E("vector", lambda v, b_=bank, o_=oc, p_=par: v.tensor_tensor(
                                    out=pre_ln[:, o_ * 512:(o_ + 1) * 512],
                                    in0=ps_c[b_][:], in1=xr[p_][:, o_ * 512:(o_ + 1) * 512],
                                    op=ALU.add),
                                    wait=wl_r, inc=ve if oc == 1 else None)
                            c_resid[it] = c_ra
                            c_b0 = E("vector", lambda v: v.bn_stats(
                                out=sca[:, 24:30], in_=pre_ln[:, 0:512]),
                                wait=[(ve, c_ra)], inc=ve)
                            c_b1x = E("vector", lambda v: v.bn_stats(
                                out=sca[:, 30:36], in_=pre_ln[:, 512:1024]),
                                wait=[(ve, c_b0)], inc=ve)
                            c_ag = E("vector", lambda v: v.bn_aggr(
                                out=sca[:, 4:6], in_=sca[:, 24:36]),
                                wait=[(ve, c_b1x)], inc=ve)
                            c_t0 = E("vector", lambda v: v.tensor_scalar(
                                out=sca[:, 6:7], in0=sca[:, 5:6], scalar1=EPS,
                                scalar2=None, op0=ALU.add),
                                wait=[(ve, c_ag)], inc=ve)
                            c_sq = E("scalar", lambda sc: sc.activation(
                                out=sca[:, 6:7], in_=sca[:, 6:7], func=AF.Sqrt),
                                wait=[(ve, c_t0)], inc=se)
                            c_rc = E("vector", lambda v: v.reciprocal(
                                out=sca[:, 7:8], in_=sca[:, 6:7]),
                                wait=[(se, c_sq)], inc=ve)
                            c_y = E("vector", lambda v: v.tensor_scalar(
                                out=ytile[:], in0=pre_ln[:],
                                scalar1=sca[:, 4:5], scalar2=sca[:, 7:8],
                                op0=ALU.subtract, op1=ALU.mult),
                                wait=[(ve, c_rc), (st_y, c_yst.get(it - 1))], inc=ve)
                            c_ys = E("sync", lambda s, it_=it: s.dma_start(
                                out=y_s[it_ * 128:(it_ + 1) * 128, :], in_=ytile[:]),
                                wait=[(ve, c_y)], inc=st_y, inc_n=16)
                            c_yst[it] = c_ys
                        phaseC = {"ve": ve.n, "se": se.n, "te": te.n,
                                  "st_y": st_y.n}

            with ExitStack() as sd:
                ed = sd.enter_context
                wq1t = ed(nc.sbuf_tensor("wq1t" + sfx, [128, DC * DFF], BF16))
                wq2t = ed(nc.sbuf_tensor("wq2t" + sfx, [128, FC * D], BF16))
                ytm = ed(nc.sbuf_tensor("ytm" + sfx, [128, 2 * D], BF16))
                yfm = ed(nc.sbuf_tensor("yfm" + sfx, [128, DC * 512], BF16))
                h16 = ed(nc.sbuf_tensor("h16" + sfx, [128, FC * 512], BF16))
                yr = [ed(nc.sbuf_tensor(f"yr{s_}_{k}", [128, D], F32)) for k in range(2)]
                outt = ed(nc.sbuf_tensor("outt" + sfx, [128, D], F32))
                outf = ed(nc.sbuf_tensor("outf" + sfx, [128, D], F32))
                ps_h = [ed(nc.psum_tensor(f"ps_h{s_}_{k}", [128, 512], F32)) for k in range(2)]
                ps_f = [ed(nc.psum_tensor(f"ps_f{s_}_{k}", [128, 512], F32)) for k in range(4)]
                ps_y = [ed(nc.psum_tensor(f"ps_y{s_}_{k}", [128, D], BF16)) for k in range(2)]

                phC = [(te, phaseC["te"]), (se, phaseC["se"]), (ve, phaseC["ve"])]
                E("gpsimd", lambda g: g.dma_start(out=wq1t[:], in_=wq1t_s),
                  wait=[(st_wq, quant_done_st)] + phC, inc=g_wq, inc_n=16)
                E("gpsimd", lambda g: g.dma_start(out=wq2t[:], in_=wq2t_s),
                  inc=g_wq, inc_n=16)
                wq_ready = g_wq.n

                c_out_prev = {}
                c_yfm = {}
                c_tylast = {}
                c_res2 = {}
                c_ost = {}
                for chn in range(NCH):
                    c_yf = None
                    c_ty = None
                    c_ty_half = {}
                    for t4 in range(4):
                        pyp = t4 % 2
                        prev_yf = c_yf
                        if t4 % 2 == 0:
                            half = t4 // 2
                            wl_yl = [(st_y, phaseC["st_y"])]
                            if chn == 0 and half == 0:
                                wl_yl += phC
                            if half == 1:
                                wl_yl.append((te, c_ty_half[0]))
                            elif chn > 0:
                                wl_yl.append((te, c_tylast[chn - 1]))
                            c_yl = E("gpsimd", lambda g, c_=chn, h_=half: g.dma_start(
                                out=ytm[:].rearrange("p (t d) -> p t d", t=2),
                                in_=y_s[c_ * 512 + h_ * 256: c_ * 512 + (h_ + 1) * 256, :].rearrange(
                                    "(t p) d -> p t d", p=128),
                                max_dma_last_dim=8192),
                                wait=wl_yl, inc=g_ytm, inc_n=16)
                        for dc in range(DC):
                            wl_ty = ()
                            if dc == 0:
                                wl_ty = [(g_ytm, c_yl)]
                                if t4 >= 2:
                                    wl_ty.append((ve, prev_yf))
                                if t4 == 0 and chn > 0:
                                    wl_ty.append((ve, c_yfm[chn - 1]))
                                    wl_ty.append((te, c_out_prev[chn - 1]))
                            c_ty = E("tensor", lambda t, p_=pyp, t_=t4, d_=dc: t.transpose(
                                out=ps_y[p_][:, d_ * 128:(d_ + 1) * 128],
                                in_=ytm[:, (t_ % 2) * D + d_ * 128: (t_ % 2) * D + (d_ + 1) * 128],
                                identity=id16[:]),
                                wait=wl_ty, inc=te if dc == DC - 1 else None)
                        if t4 % 2 == 1:
                            c_ty_half[t4 // 2] = c_ty
                        c_yf = E("vector", lambda v, p_=pyp, t_=t4: v.tensor_copy(
                            out=yfm[:].rearrange("p (c n) -> p c n", c=DC)[:, :, t_ * 128:(t_ + 1) * 128],
                            in_=ps_y[p_][:].rearrange("p (c n) -> p c n", c=DC)),
                            wait=[(te, c_ty)], inc=ve)
                    c_yfm[chn] = c_yf
                    c_tylast[chn] = c_ty

                    c_relu = {}
                    for fc in range(FC):
                        hpar = fc % 2
                        c_m = None
                        for dc in range(DC):
                            wl = ()
                            if dc == 0:
                                wl = [(ve, c_yf), (g_wq, wq_ready)]
                                prev = c_relu.get(fc - 2)
                                if prev is not None:
                                    wl.append(prev)
                                if chn > 0 and fc < 2:
                                    wl.append((te, c_out_prev[chn - 1]))
                            c_m = E("tensor", lambda t, p_=hpar, f_=fc, d_=dc: t.matmul(
                                out=ps_h[p_][:],
                                lhsT=wq1t[:, d_ * DFF + f_ * 128: d_ * DFF + (f_ + 1) * 128],
                                rhs=yfm[:, d_ * 512:(d_ + 1) * 512],
                                start=(d_ == 0), stop=(d_ == DC - 1), skip_group_check=True),
                                wait=wl, inc=te if dc == DC - 1 else None)
                        if fc % 2 == 0:
                            c_r = E("vector", lambda v, p_=hpar, f_=fc: v.tensor_scalar(
                                out=h16[:, f_ * 512:(f_ + 1) * 512], in0=ps_h[p_][:],
                                scalar1=b1T[:, f_:f_ + 1], scalar2=0.0,
                                op0=ALU.add, op1=ALU.max),
                                wait=[(te, c_m)], inc=ve)
                            c_relu[fc] = (ve, c_r)
                        else:
                            c_r = E("scalar", lambda sc, p_=hpar, f_=fc: sc.activation(
                                out=h16[:, f_ * 512:(f_ + 1) * 512], in_=ps_h[p_][:],
                                func=AF.Relu, bias=b1T[:, f_:f_ + 1]),
                                wait=[(te, c_m)], inc=se)
                            c_relu[fc] = (se, c_r)
                    h_ready = [c_relu[FC - 2], c_relu[FC - 1]]

                    c_mm2l = None
                    for it4 in range(4):
                        it = chn * 4 + it4
                        par = it4 % 2
                        wl_y2 = [(st_y, phaseC["st_y"]), (ve, c_res2.get(it - 2))]
                        if it < 2:
                            wl_y2 += phC
                        E("sync", lambda s, it_=it, p_=par: s.dma_start(
                            out=yr[p_][:], in_=y_s[it_ * 128:(it_ + 1) * 128, :]),
                            wait=wl_y2, inc=ldr[par], inc_n=16)
                        nldr2 = ldr[par].n
                        c_mm2 = {}
                        for oc in range(2):
                            bank = 2 * par + oc
                            wl = list(h_ready) + [(ve, c_res2.get(it - 2))]
                            E("tensor", lambda t, b_=bank, o_=oc: t.matmul(
                                out=ps_f[b_][:], lhsT=ones16[:],
                                rhs=b2_16[0:1, o_ * 512:(o_ + 1) * 512],
                                start=True, stop=False, skip_group_check=True),
                                wait=wl)
                            c_q = None
                            for fc in range(FC):
                                c_q = E("tensor", lambda t, b_=bank, o_=oc, f_=fc, i_=it4: t.matmul(
                                    out=ps_f[b_][:],
                                    lhsT=h16[:, f_ * 512 + i_ * 128: f_ * 512 + i_ * 128 + 128],
                                    rhs=wq2t[:, f_ * D + o_ * 512: f_ * D + (o_ + 1) * 512],
                                    start=False, stop=(f_ == FC - 1), skip_group_check=True),
                                    inc=te if fc == FC - 1 else None)
                            c_mm2[oc] = c_q
                        c_mm2l = c_mm2[1]
                        c_ra = None
                        for oc in range(2):
                            bank = 2 * par + oc
                            wl_r = [(te, c_mm2[oc])]
                            if oc == 0:
                                wl_r.append((ldr[par], nldr2))
                            c_ra = E("vector", lambda v, b_=bank, o_=oc, p_=par: v.tensor_tensor(
                                out=outt[:, o_ * 512:(o_ + 1) * 512],
                                in0=ps_f[b_][:], in1=yr[p_][:, o_ * 512:(o_ + 1) * 512],
                                op=ALU.add),
                                wait=wl_r, inc=ve if oc == 1 else None)
                        c_b0 = E("vector", lambda v: v.bn_stats(
                            out=sca[:, 24:30], in_=outt[:, 0:512]),
                            wait=[(ve, c_ra)], inc=ve)
                        c_b1x = E("vector", lambda v: v.bn_stats(
                            out=sca[:, 30:36], in_=outt[:, 512:1024]),
                            wait=[(ve, c_b0)], inc=ve)
                        c_ag = E("vector", lambda v: v.bn_aggr(
                            out=sca[:, 4:6], in_=sca[:, 24:36]),
                            wait=[(ve, c_b1x)], inc=ve)
                        c_t0 = E("vector", lambda v: v.tensor_scalar(
                            out=sca[:, 6:7], in0=sca[:, 5:6], scalar1=EPS,
                            scalar2=None, op0=ALU.add),
                            wait=[(ve, c_ag)], inc=ve)
                        c_sq = E("scalar", lambda sc: sc.activation(
                            out=sca[:, 6:7], in_=sca[:, 6:7], func=AF.Sqrt),
                            wait=[(ve, c_t0)], inc=se)
                        c_rc = E("vector", lambda v: v.reciprocal(
                            out=sca[:, 7:8], in_=sca[:, 6:7]),
                            wait=[(se, c_sq)], inc=ve)
                        c_o = E("vector", lambda v: v.tensor_scalar(
                            out=outf[:], in0=outt[:],
                            scalar1=sca[:, 4:5], scalar2=sca[:, 7:8],
                            op0=ALU.subtract, op1=ALU.mult),
                            wait=[(ve, c_rc), (st_out, c_ost.get(it - 1))], inc=ve)
                        c_res2[it] = c_o
                        c_ost[it] = E("sync", lambda s, it_=it, sq=s_: s.dma_start(
                            out=out[sq, it_ * 128:(it_ + 1) * 128, :], in_=outf[:]),
                            wait=[(ve, c_o)], inc=st_out, inc_n=16)
                    c_out_prev[chn] = c_mm2l
                prev_end = {"ve": ve.n, "se": se.n, "te": te.n,
                            "st_wq": quant_done_st}

        E("sync", lambda s: s.nop(), wait=[(st_out, st_out.n)])

        with nc.Block() as block:
            P.realize(block)

    return nc



def _consts():
    i = np.arange(128)[:, None]
    c = np.arange(512)[None, :]
    masks = []
    for k in range(4):
        m = ((128 * k + i) >= c).astype(np.float32)
        masks += [m, m]
    cmask = np.concatenate(masks, axis=1).astype(ml_dtypes.bfloat16)
    id32 = np.eye(128, dtype=np.float32)
    id16 = np.eye(128, dtype=np.float32).astype(ml_dtypes.bfloat16)
    return cmask, id32, id16


def make_in_map(xshard, w_qkv, b_qkv, w_out, b_out, W1, b1, W2, b2):
    cmask, id32, id16 = _consts()
    return {
        "x": np.ascontiguousarray(xshard, dtype=np.float32),
        "w_qkv": np.ascontiguousarray(w_qkv, dtype=np.float32),
        "w_out": np.ascontiguousarray(w_out, dtype=np.float32),
        "W1": np.ascontiguousarray(W1, dtype=np.float32),
        "W2": np.ascontiguousarray(W2, dtype=np.float32),
        "bqkv_blk": np.ascontiguousarray(
            np.asarray(b_qkv, dtype=np.float32).reshape(24, 128)),
        "bqkv_row": np.ascontiguousarray(
            np.asarray(b_qkv, dtype=np.float32).reshape(1, 3 * D)),
        "b1_blk": np.ascontiguousarray(
            np.asarray(b1, dtype=np.float32).reshape(32, 128)),
        "bout_row": np.ascontiguousarray(
            np.asarray(b_out, dtype=np.float32).reshape(1, D)),
        "b2_row": np.ascontiguousarray(
            np.asarray(b2, dtype=np.float32).reshape(1, D)),
        "cmask": cmask, "id32": id32, "id16": id16,
    }


def kernel(L0_attn_x, L0_attn_w_qkv, L0_attn_b_qkv, L0_attn_w_out,
           L0_attn_b_out, L0_ffn_W1, L0_ffn_b1, L0_ffn_W2, L0_ffn_b2):
    x = np.ascontiguousarray(np.asarray(L0_attn_x, dtype=np.float32))
    B = x.shape[0]
    per = B // NCORES
    nc = build_core_program(per)
    in_maps = []
    for c in range(NCORES):
        in_maps.append(make_in_map(
            x[c * per:(c + 1) * per], L0_attn_w_qkv, L0_attn_b_qkv,
            L0_attn_w_out, L0_attn_b_out, L0_ffn_W1, L0_ffn_b1,
            L0_ffn_W2, L0_ffn_b2))
    res = run_bass_kernel_spmd(nc, in_maps, core_ids=list(range(NCORES)))
    return np.concatenate([np.asarray(r["out"]) for r in res.results], axis=0)


# version 05
# speedup vs baseline: 1.4229x; 1.0557x over previous
import numpy as np
import ml_dtypes
from contextlib import ExitStack

import concourse.bass as bass
import concourse.mybir as mybir
from concourse.bass_utils import run_bass_kernel_spmd

F32 = mybir.dt.float32
BF16 = mybir.dt.bfloat16
AF = mybir.ActivationFunctionType
ALU = mybir.AluOpType
AX = mybir.AxisListType

NCORES = 8
S = 2048
D = 1024
H = 16
HD = 64
HPAIRS = 8
DFF = 4096
NT = S // 128
NCH = S // 512
DC = D // 128
FC = DFF // 128
EPS = 1e-5
SCALE = 0.125

ENGINES = ("tensor", "vector", "scalar", "gpsimd", "sync")


class Sem:
    def __init__(self, nc, stack, name):
        self.h = stack.enter_context(nc.semaphore(name))
        self.n = 0
        self.name = name


class Prog:
    def __init__(self):
        self.ops = {e: [] for e in ENGINES}

    def emit(self, eng, fn, wait=(), inc=None, inc_n=1):
        cnt = None
        if inc is not None:
            inc.n += inc_n
            cnt = inc.n
        w = tuple((s, c) for (s, c) in wait if s is not None and c)
        self.ops[eng].append((w, fn, inc, inc_n))
        return cnt

    def realize(self, block):
        for eng in ENGINES:
            ops = self.ops[eng]
            if not ops:
                continue

            def body(E, _ops=ops):
                for w, fn, inc, inc_n in _ops:
                    for (sem, cnt) in w:
                        E.wait_ge(sem.h, cnt)
                    instr = fn(E)
                    if inc is not None:
                        instr.then_inc(inc.h, inc_n)

            getattr(block, eng)(body)


def build_core_program(nseq=4):
    nc = bass.Bass("TRN2", target_bir_lowering=False)

    x_in = nc.dram_tensor("x", [nseq, S, D], F32, kind="ExternalInput").ap()
    w_qkv = nc.dram_tensor("w_qkv", [D, 3 * D], F32, kind="ExternalInput").ap()
    w_out = nc.dram_tensor("w_out", [D, D], F32, kind="ExternalInput").ap()
    W1 = nc.dram_tensor("W1", [DFF, D], F32, kind="ExternalInput").ap()
    W2 = nc.dram_tensor("W2", [D, DFF], F32, kind="ExternalInput").ap()
    bqkv_blk = nc.dram_tensor("bqkv_blk", [24, 128], F32, kind="ExternalInput").ap()
    bqkv_row = nc.dram_tensor("bqkv_row", [1, 3 * D], F32, kind="ExternalInput").ap()
    b1_blk = nc.dram_tensor("b1_blk", [32, 128], F32, kind="ExternalInput").ap()
    bout_row = nc.dram_tensor("bout_row", [1, D], F32, kind="ExternalInput").ap()
    b2_row = nc.dram_tensor("b2_row", [1, D], F32, kind="ExternalInput").ap()
    cmask_in = nc.dram_tensor("cmask", [128, 4 * 1024], BF16, kind="ExternalInput").ap()
    id32_in = nc.dram_tensor("id32", [128, 128], F32, kind="ExternalInput").ap()
    id16_in = nc.dram_tensor("id16", [128, 128], BF16, kind="ExternalInput").ap()

    wq1t_s = nc.dram_tensor("wq1t_s", [128, DC * DFF], BF16, kind="Internal").ap()
    wq2t_s = nc.dram_tensor("wq2t_s", [128, FC * D], BF16, kind="Internal").ap()
    y_s = nc.dram_tensor("y_s", [S, D], F32, kind="Internal").ap()

    out = nc.dram_tensor("out", [nseq, S, D], F32, kind="ExternalOutput").ap()

    P = Prog()

    with ExitStack() as top:
        et = top.enter_context

        cmask = et(nc.sbuf_tensor("cmask_sb", [128, 4 * 1024], BF16))
        id32 = et(nc.sbuf_tensor("id32_sb", [128, 128], F32))
        id16 = et(nc.sbuf_tensor("id16_sb", [128, 128], BF16))
        ones16 = et(nc.sbuf_tensor("ones16", [1, 128], BF16))
        onesc32 = et(nc.sbuf_tensor("onesc32", [128, 1], F32))
        ones32 = et(nc.sbuf_tensor("ones32", [1, 128], F32))
        sca = et(nc.sbuf_tensor("sca", [128, 40], F32))

        sems = {}
        for name in ("te", "ve", "se", "ldc", "ldx0", "ldx1", "ldx2", "ldx3",
                     "lW", "st_wq", "g_wqkv", "g_wout", "g_wq", "g_ytm",
                     "ldr0", "ldr1", "st_y", "st_out"):
            sems[name] = Sem(nc, top, name)
        te, ve, se = sems["te"], sems["ve"], sems["se"]
        ldc = sems["ldc"]
        ldx = (sems["ldx0"], sems["ldx1"], sems["ldx2"], sems["ldx3"])
        lW = sems["lW"]
        st_wq = sems["st_wq"]
        g_wqkv, g_wout, g_wq, g_ytm = (sems["g_wqkv"], sems["g_wout"],
                                       sems["g_wq"], sems["g_ytm"])
        ldr = (sems["ldr0"], sems["ldr1"])
        st_y, st_out = sems["st_y"], sems["st_out"]

        E = P.emit

        if True:
            for src, dst in ((cmask_in, cmask), (id32_in, id32), (id16_in, id16)):
                E("sync", lambda s, a=src, b=dst: s.dma_start(out=b[:], in_=a),
                  inc=ldc, inc_n=16)
            c_all = ldc.n
            E("vector", lambda v: v.memset(ones16[:], 1.0))
            E("vector", lambda v: v.memset(onesc32[:], 1.0))
            c_ones = E("vector", lambda v: v.memset(ones32[:], 1.0),
                       wait=[(ldc, c_all)], inc=ve)

        if True:
            def quantize(widx, W_hbm, nchunks_in, in_chunk, out_cols, scratch,
                         thr_col, nthr_col, extra_wait):
                with ExitStack() as qs:
                    eq = qs.enter_context
                    wf = eq(nc.sbuf_tensor(f"wf{widx}", [128, nchunks_in * in_chunk], F32))
                    stg = eq(nc.sbuf_tensor(f"stg{widx}", [128, out_cols], BF16))
                    red = eq(nc.sbuf_tensor(f"red{widx}", [128, 256], F32))
                    nbuf = eq(nc.sbuf_tensor(f"nbuf{widx}", [128, 128], F32))
                    ps_q = eq(nc.psum_tensor(f"ps_q{widx}", [128, 128], F32))
                    ps_tr = [eq(nc.psum_tensor(f"ps_tq{k}_{widx}", [128, 128], F32))
                             for k in range(2)]

                    c_w = E("sync", lambda s: s.dma_start(
                        out=wf[:].rearrange("p (c n) -> p c n", c=nchunks_in),
                        in_=W_hbm.rearrange("(c p) n -> p c n", p=128)),
                        wait=[(st_wq, st_wq.n)] + list(extra_wait),
                        inc=lW, inc_n=16)

                    nel = 128 * nchunks_in * in_chunk
                    c_r1 = E("vector", lambda v: v.tensor_reduce(
                        out=red[:], in_=wf[:].rearrange("p (a b) -> p a b", a=256),
                        axis=AX.X, op=ALU.add, apply_absolute_value=True),
                        wait=[(lW, c_w)], inc=ve)
                    c_r2 = E("vector", lambda v: v.tensor_reduce(
                        out=red[:, 0:1], in_=red[:], axis=AX.X, op=ALU.add,
                        apply_absolute_value=True),
                        wait=[(ve, c_r1)], inc=ve)
                    c_m1 = E("tensor", lambda t: t.matmul(
                        out=ps_q[0:1, 0:1], lhsT=onesc32[:], rhs=red[:, 0:1],
                        start=True, stop=True, skip_group_check=True),
                        wait=[(ve, c_r2)], inc=te)
                    c_g = E("vector", lambda v: v.tensor_copy(
                        out=sca[0:1, 20:21], in_=ps_q[0:1, 0:1]),
                        wait=[(te, c_m1)], inc=ve)
                    c_m2 = E("tensor", lambda t: t.matmul(
                        out=ps_q[:, 0:1], lhsT=ones32[:], rhs=sca[0:1, 20:21],
                        start=True, stop=True, skip_group_check=True),
                        wait=[(ve, c_g)], inc=te)
                    c_thr = E("vector", lambda v: v.tensor_scalar(
                        out=sca[:, thr_col:thr_col + 1], in0=ps_q[:, 0:1],
                        scalar1=0.5 / nel, scalar2=0.5 * EPS, op0=ALU.mult, op1=ALU.add),
                        wait=[(te, c_m2)], inc=ve)
                    c_nthr = E("vector", lambda v: v.tensor_scalar(
                        out=sca[:, nthr_col:nthr_col + 1], in0=sca[:, thr_col:thr_col + 1],
                        scalar1=-1.0, scalar2=None, op0=ALU.mult),
                        wait=[(ve, c_thr)], inc=ve)

                    nblk_out = DC if widx == 1 else FC
                    nblk_in = FC if widx == 1 else DC
                    last_stt = c_nthr
                    prev_stt = {}
                    for bi in range(nblk_in):
                        for bo in range(nblk_out):
                            par = (bi * nblk_out + bo) % 2
                            src = wf[:, bi * in_chunk + bo * 128: bi * in_chunk + bo * 128 + 128]
                            c_tp = E("tensor", lambda t, s_=src, p_=ps_tr[par]: t.transpose(
                                out=p_[:], in_=s_, identity=id32[:]),
                                wait=[(lW, c_w), (ve, c_nthr),
                                      (ve, prev_stt.get(par))], inc=te)
                            c_n = E("vector", lambda v, p_=ps_tr[par]: v.tensor_scalar(
                                out=nbuf[:], in0=p_[:],
                                scalar1=sca[:, nthr_col:nthr_col + 1], scalar2=None,
                                op0=ALU.is_lt),
                                wait=[(te, c_tp), (ve, last_stt)], inc=ve)
                            if widx == 1:
                                dst = stg[:, bo * DFF + bi * 128: bo * DFF + bi * 128 + 128]
                            else:
                                dst = stg[:, bo * D + bi * 128: bo * D + bi * 128 + 128]
                            c_stt = E("vector", lambda v, p_=ps_tr[par], d_=dst: v.scalar_tensor_tensor(
                                out=d_, in0=p_[:], scalar=sca[:, thr_col:thr_col + 1],
                                in1=nbuf[:], op0=ALU.is_gt, op1=ALU.subtract),
                                wait=[(ve, c_n)], inc=ve)
                            prev_stt[par] = c_stt
                            last_stt = c_stt
                    E("sync", lambda s: s.dma_start(out=scratch, in_=stg[:]),
                      wait=[(ve, last_stt)], inc=st_wq, inc_n=16)

            quantize(1, W1, FC, D, DC * DFF, wq1t_s, 0, 1, [])
            quantize(2, W2, DC, DFF, FC * D, wq2t_s, 2, 3, [])
            quant_done_st = st_wq.n

        bqkvT = et(nc.sbuf_tensor("bqkvT", [128, 24], F32))
        b1T = et(nc.sbuf_tensor("b1T", [128, 32], F32))
        bv16 = et(nc.sbuf_tensor("bv16", [1, D], BF16))
        bout16 = et(nc.sbuf_tensor("bout16", [1, D], BF16))
        b2_16 = et(nc.sbuf_tensor("b2_16", [1, D], BF16))
        with ExitStack() as ist:
            ei = ist.enter_context
            btmp = ei(nc.sbuf_tensor("btmp", [32, 128], F32))
            brow = ei(nc.sbuf_tensor("brow", [1, 3 * D], F32))
            ps_i = ei(nc.psum_tensor("ps_i", [128, 64], F32))
            qve = ve.n
            E("sync", lambda s: s.dma_start(out=brow[:], in_=bqkv_row),
              wait=[(st_wq, quant_done_st), (ve, qve)], inc=ldc, inc_n=16)
            c_bv = E("vector", lambda v: v.tensor_copy(out=bv16[:], in_=brow[0:1, 2 * D:3 * D]),
                     wait=[(ldc, ldc.n)], inc=ve)
            E("sync", lambda s: s.dma_start(out=btmp[0:24, :], in_=bqkv_blk),
              wait=[(ve, c_bv)], inc=ldc, inc_n=16)
            c_b1 = ldc.n
            c_t1 = E("tensor", lambda t: t.transpose(out=ps_i[:, 0:24], in_=btmp[0:24, :], identity=id32[0:24, 0:24]),
                     wait=[(ldc, c_b1), (ve, qve)], inc=te)
            c_e1 = E("vector", lambda v: v.tensor_copy(out=bqkvT[:], in_=ps_i[:, 0:24]),
                     wait=[(te, c_t1)], inc=ve)
            E("sync", lambda s: s.dma_start(out=btmp[0:32, :], in_=b1_blk),
              wait=[(ve, c_e1)], inc=ldc, inc_n=16)
            c_b2 = ldc.n
            c_t2 = E("tensor", lambda t: t.transpose(out=ps_i[:, 32:64], in_=btmp[0:32, :], identity=id32[0:32, 0:32]),
                     wait=[(ldc, c_b2)], inc=te)
            c_e2 = E("vector", lambda v: v.tensor_copy(out=b1T[:], in_=ps_i[:, 32:64]),
                     wait=[(te, c_t2)], inc=ve)
            E("sync", lambda s: s.dma_start(out=brow[0:1, 0:D], in_=bout_row),
              wait=[(ve, c_e2)], inc=ldc, inc_n=16)
            c_bo = E("vector", lambda v: v.tensor_copy(out=bout16[:], in_=brow[0:1, 0:D]),
                     wait=[(ldc, ldc.n)], inc=ve)
            E("sync", lambda s: s.dma_start(out=brow[0:1, 0:D], in_=b2_row),
              wait=[(ve, c_bo)], inc=ldc, inc_n=16)
            c_binit = E("vector", lambda v: v.tensor_copy(out=b2_16[:], in_=brow[0:1, 0:D]),
                        wait=[(ldc, ldc.n)], inc=ve)

        prev_end = {"ve": ve.n, "se": se.n, "te": te.n, "st_wq": quant_done_st}

        for s_ in range(nseq):
            sfx = f"_{s_}"
            with ExitStack() as sab:
                ea = sab.enter_context
                Qf = ea(nc.sbuf_tensor("Qf" + sfx, [128, DC * S], BF16))
                Kf = ea(nc.sbuf_tensor("Kf" + sfx, [128, DC * S], BF16))
                Vaug = ea(nc.sbuf_tensor("Vaug" + sfx, [128, NT * 1040], BF16))

                with ExitStack() as sa:
                    e2 = sa.enter_context
                    xfm = e2(nc.sbuf_tensor("xfm" + sfx, [128, DC * S], BF16))
                    wq16 = e2(nc.sbuf_tensor("wq16" + sfx, [128, DC * 3 * D], BF16))
                    xin = [e2(nc.sbuf_tensor(f"xin{s_}_{k}", [128, D], BF16)) for k in range(4)]
                    ps_xt = [e2(nc.psum_tensor(f"ps_xt{s_}_{k}", [128, D], BF16)) for k in range(2)]
                    ps_qk = [e2(nc.psum_tensor(f"ps_qk{s_}_{k}", [128, 512], F32)) for k in range(2)]
                    ps_v = [e2(nc.psum_tensor(f"ps_v{s_}_{k}", [128, 512], F32)) for k in range(2)]

                    for dcw in range(DC):
                        E("gpsimd", lambda g, d_=dcw: g.dma_start(
                            out=wq16[:, d_ * 3 * D:(d_ + 1) * 3 * D],
                            in_=w_qkv[d_ * 128:(d_ + 1) * 128, :],
                            max_dma_last_dim=8192),
                            wait=([(ve, prev_end["ve"]), (se, prev_end["se"]),
                                   (te, prev_end["te"]), (st_wq, prev_end["st_wq"])]
                                  if dcw == 0 else ()),
                            inc=g_wqkv, inc_n=16)
                    wqkv_ready = g_wqkv.n

                    v_ready = []
                    xfm_ready = []
                    c_evac = {}
                    c_trans = {}
                    c_vevac = {}
                    for it in range(NT):
                        par = it % 2
                        buf = it % 4
                        wl_load = [(te, c_trans.get(it - 4))]
                        if it < 4:
                            wl_load += [(ve, prev_end["ve"]), (se, prev_end["se"]),
                                        (te, prev_end["te"]),
                                        (st_wq, prev_end["st_wq"])]
                        E("gpsimd", lambda g, it_=it, b_=buf, sq=s_: g.dma_start(
                            out=xin[b_][:], in_=x_in[sq, it_ * 128:(it_ + 1) * 128, :]),
                            wait=wl_load, inc=ldx[it % 4], inc_n=16)
                        nld = ldx[it % 4].n
                        c_t = None
                        for dc in range(DC):
                            wl_t = []
                            if dc == 0:
                                wl_t = [(ldx[it % 4], nld), (ve, c_evac.get(it - 2))]
                                if it < 2:
                                    wl_t.append((ve, prev_end["ve"]))
                            c_t = E("tensor", lambda t, p_=par, b_=buf, d_=dc: t.transpose(
                                out=ps_xt[p_][:, d_ * 128:(d_ + 1) * 128],
                                in_=xin[b_][:, d_ * 128:(d_ + 1) * 128],
                                identity=id16[:]),
                                wait=wl_t, inc=te if dc == DC - 1 else None)
                        c_trans[it] = c_t
                        c_e = E("vector", lambda v, p_=par, it_=it: v.tensor_copy(
                            out=xfm[:].rearrange("p (c n) -> p c n", c=DC)[:, :, it_ * 128:(it_ + 1) * 128],
                            in_=ps_xt[p_][:].rearrange("p (c n) -> p c n", c=DC)),
                            wait=[(te, c_trans[it])], inc=ve)
                        c_evac[it] = c_e
                        xfm_ready.append(c_e)
                        E("vector", lambda v, it_=it: v.memset(
                            Vaug[:, it_ * 1040:(it_ + 1) * 1040].rearrange(
                                "p (g c) -> p g c", c=65)[:, :, 64:65], 1.0))
                        for vc in range(2):
                            vpar = (it * 2 + vc) % 2
                            E("tensor", lambda t, p_=vpar, v_=vc: t.matmul(
                                out=ps_v[p_][:], lhsT=ones16[:],
                                rhs=bv16[0:1, v_ * 512:(v_ + 1) * 512],
                                start=True, stop=False, skip_group_check=True),
                                wait=[(ve, c_e), (g_wqkv, wqkv_ready),
                                      (ve, c_vevac.get((it * 2 + vc) - 2))])
                            c_mm = None
                            for dc in range(DC):
                                c_mm = E("tensor", lambda t, p_=vpar, v_=vc, d_=dc, it_=it: t.matmul(
                                    out=ps_v[p_][:],
                                    lhsT=xfm[:, d_ * S + it_ * 128: d_ * S + it_ * 128 + 128],
                                    rhs=wq16[:, d_ * 3 * D + 2 * D + v_ * 512: d_ * 3 * D + 2 * D + (v_ + 1) * 512],
                                    start=False, stop=(d_ == DC - 1), skip_group_check=True),
                                    inc=te if dc == DC - 1 else None)
                            c_vv = E("vector", lambda v, p_=vpar, v2=vc, it_=it: v.tensor_copy(
                                out=Vaug[:, it_ * 1040 + v2 * 8 * 65: it_ * 1040 + (v2 * 8 + 8) * 65].rearrange(
                                    "p (g c) -> p g c", c=65)[:, :, 0:64],
                                in_=ps_v[p_][:].rearrange("p (g c) -> p g c", c=64)),
                                wait=[(te, c_mm)], inc=ve)
                            c_vevac[it * 2 + vc] = c_vv
                        v_ready.append(c_vevac[it * 2 + 1])

                    qk_ready = []
                    c_qkevac = {}
                    for ch in range(NCH):
                        for qc in range(16):
                            qpar = (ch * 16 + qc) % 2
                            c_mm = None
                            for dc in range(DC):
                                wl_q = ()
                                if dc == 0:
                                    wl_q = [(ve, xfm_ready[4 * ch + 3]),
                                            (g_wqkv, wqkv_ready),
                                            (se, c_qkevac.get((ch * 16 + qc) - 2))]
                                c_mm = E("tensor", lambda t, p_=qpar, q_=qc, d_=dc, c_=ch: t.matmul(
                                    out=ps_qk[p_][:],
                                    lhsT=wq16[:, d_ * 3 * D + q_ * 128: d_ * 3 * D + (q_ + 1) * 128],
                                    rhs=xfm[:, d_ * S + c_ * 512: d_ * S + (c_ + 1) * 512],
                                    start=(d_ == 0), stop=(d_ == DC - 1), skip_group_check=True),
                                    wait=wl_q, inc=te if dc == DC - 1 else None)
                            dstbuf = Qf if qc < 8 else Kf
                            dcol = (qc % 8) * S + ch * 512
                            c_sev = E("scalar", lambda sc, p_=qpar, b_=dstbuf, col=dcol, q_=qc: sc.activation(
                                out=b_[:, col:col + 512], in_=ps_qk[p_][:],
                                func=AF.Identity, bias=bqkvT[:, q_:q_ + 1]),
                                wait=[(te, c_mm)], inc=se)
                            c_qkevac[ch * 16 + qc] = c_sev
                        qk_ready.append(c_qkevac[ch * 16 + 15])
                    phaseA = {"ve": ve.n, "se": se.n, "te": te.n}

                with ExitStack() as sb:
                    eb = sb.enter_context
                    Ofm = eb(nc.sbuf_tensor("Ofm" + sfx, [128, HPAIRS * S], BF16))
                    w_out16 = eb(nc.sbuf_tensor("wout16" + sfx, [128, DC * D], BF16))

                    E("gpsimd", lambda g: g.dma_start(
                        out=w_out16[:].rearrange("p (c n) -> p c n", c=DC),
                        in_=w_out.rearrange("(c p) n -> p c n", p=128)),
                        wait=[(se, phaseA["se"]), (te, phaseA["te"]),
                              (ve, phaseA["ve"])],
                        inc=g_wout, inc_n=16)
                    wout_ready = g_wout.n

                    with ExitStack() as sbp:
                        ep = sbp.enter_context
                        Pb = [ep(nc.sbuf_tensor(f"Pb{s_}_{k}", [128, 1024], BF16)) for k in range(2)]
                        Pd = [ep(nc.sbuf_tensor(f"Pd{s_}_{k}", [128, 1024], BF16)) for k in range(2)]
                        of32 = ep(nc.sbuf_tensor("of32" + sfx, [128, 1024], F32))
                        otm = ep(nc.sbuf_tensor("otm" + sfx, [128, 512], BF16))
                        ps_s2 = [ep(nc.psum_tensor(f"ps_s{s_}_{k}", [128, 1024], F32)) for k in range(2)]
                        ps_o = ep(nc.psum_tensor("ps_o" + sfx, [128, 1024], F32))
                        ps_me = ep(nc.psum_tensor("ps_me" + sfx, [128, 512], F32))
                        ps_mo = ep(nc.psum_tensor("ps_mo" + sfx, [128, 512], F32))
                        tb_e = ps_me[:].bitcast(BF16)
                        tb_o = ps_mo[:].bitcast(BF16)

                        prev = {"c_c1": None, "c_tt8": None, "c_r1": None,
                                "norm_se": None, "c_tb8": None, "vofm": None,
                                "last_exp": None, "last_pv": None}
                        tail = [None]

                        def emit_S(hp, ch, jt, jdx, c_exp, first_wl):
                            sl = jdx % 2
                            wl = list(first_wl)
                            if jdx >= 2:
                                wl.append((se, c_exp[jt - 2]))
                            E("tensor", lambda t, h_=hp, c_=ch, j_=jt, s0=sl: t.matmul(
                                out=ps_s2[s0][:, 0:512],
                                lhsT=Kf[0:64, h_ * S + j_ * 128: h_ * S + (j_ + 1) * 128],
                                rhs=Qf[0:64, h_ * S + c_ * 512: h_ * S + (c_ + 1) * 512],
                                start=True, stop=True, skip_group_check=True,
                                tile_position=(0, 0)),
                                wait=wl)
                            return E("tensor", lambda t, h_=hp, c_=ch, j_=jt, s0=sl: t.matmul(
                                out=ps_s2[s0][:, 512:1024],
                                lhsT=Kf[64:128, h_ * S + j_ * 128: h_ * S + (j_ + 1) * 128],
                                rhs=Qf[64:128, h_ * S + c_ * 512: h_ * S + (c_ + 1) * 512],
                                start=True, stop=True, skip_group_check=True,
                                tile_position=(64, 0)),
                                inc=te)

                        for hp in range(HPAIRS):
                            for ch in range(NCH):
                                jt0 = 4 * ch
                                n = NT - jt0
                                c_s = {}
                                c_exp = {}
                                c_pvl = {}
                                c_mk = {}
                                base_wl = []
                                if prev["last_exp"] is not None:
                                    base_wl.append((se, prev["last_exp"]))
                                else:
                                    base_wl += [(te, phaseA["te"]), (ve, phaseA["ve"]),
                                                (se, phaseA["se"])]
                                base_wl.append((se, qk_ready[ch]))
                                c_s[jt0] = emit_S(hp, ch, jt0, 0, c_exp,
                                                  base_wl + [(ve, v_ready[jt0])])
                                if n > 1:
                                    c_s[jt0 + 1] = emit_S(hp, ch, jt0 + 1, 1, c_exp,
                                                          base_wl + [(ve, v_ready[jt0 + 1])])
                                for jdx in range(n):
                                    jt = jt0 + jdx
                                    sl = jdx % 2
                                    wl2 = [(te, c_s[jt])]
                                    if jdx >= 2:
                                        wl2.append((te, c_pvl[jt - 2]))
                                    elif prev["last_pv"] is not None:
                                        wl2.append((te, prev["last_pv"]))
                                    c_e2 = E("scalar", lambda sc, s0=sl: sc.activation(
                                        out=Pb[s0][:], in_=ps_s2[s0][:],
                                        func=AF.Exp, scale=SCALE),
                                        wait=wl2, inc=se)
                                    c_exp[jt] = c_e2
                                    if jdx < 4:
                                        wl_m = [(se, c_e2)]
                                        pdw = c_pvl.get(jt - 2, prev["last_pv"])
                                        if pdw is not None:
                                            wl_m.append((te, pdw))
                                        c_m = E("vector", lambda v, s0=sl, k_=jdx, pi=jdx % 2: v.tensor_tensor(
                                            out=Pd[pi][:], in0=Pb[s0][:],
                                            in1=cmask[:, k_ * 1024:(k_ + 1) * 1024],
                                            op=ALU.mult),
                                            wait=wl_m, inc=ve)
                                        c_mk[jt] = c_m
                                        src = Pd[jdx % 2]
                                        wl3 = [(ve, c_m)]
                                    else:
                                        src = Pb[sl]
                                        wl3 = [(se, c_e2)]
                                    if jdx == 0:
                                        if prev["c_c1"] is not None:
                                            wl3.append((ve, prev["c_c1"]))
                                        else:
                                            wl3.append((ve, phaseA["ve"]))
                                    E("tensor", lambda t, h_=hp, j_=jt, s2=src, f_=(jdx == 0), l_=(jdx == n - 1): t.matmul(
                                        out=ps_o[0:65, 0:512],
                                        lhsT=Vaug[:, j_ * 1040 + (2 * h_) * 65: j_ * 1040 + (2 * h_ + 1) * 65],
                                        rhs=s2[:, 0:512], start=f_, stop=l_,
                                        skip_group_check=True),
                                        wait=wl3)
                                    c_p = E("tensor", lambda t, h_=hp, j_=jt, s2=src, f_=(jdx == 0), l_=(jdx == n - 1): t.matmul(
                                        out=ps_o[0:65, 512:1024],
                                        lhsT=Vaug[:, j_ * 1040 + (2 * h_ + 1) * 65: j_ * 1040 + (2 * h_ + 2) * 65],
                                        rhs=s2[:, 512:1024], start=f_, stop=l_,
                                        skip_group_check=True),
                                        inc=te)
                                    c_pvl[jt] = c_p
                                    if jdx + 2 < n:
                                        c_s[jt + 2] = emit_S(hp, ch, jt + 2, jdx + 2, c_exp,
                                                             [(ve, v_ready[jt + 2])])
                                    if jdx == 0 and tail[0] is not None:
                                        tail[0]()
                                        tail[0] = None
                                last_pv = c_pvl[jt0 + n - 1]
                                last_exp = c_exp[jt0 + n - 1]

                                wl_c = [(te, last_pv)]
                                if prev["c_tt8"] is not None:
                                    wl_c.append((te, prev["c_tt8"]))
                                c_c1 = E("vector", lambda v: v.tensor_copy(
                                    out=of32[0:65, :], in_=ps_o[0:65, :]),
                                    wait=wl_c, inc=ve)
                                c_tt = None
                                for head in range(2):
                                    pt = ps_me if head == 0 else ps_mo
                                    for sub in range(4):
                                        wl_tt = ()
                                        if head == 0 and sub == 0:
                                            wl_tt = [(ve, c_c1)]
                                            if prev["norm_se"] is not None:
                                                wl_tt += [(se, prev["norm_se"]),
                                                          (ve, prev["c_r1"])]
                                        c_tt = E("tensor", lambda t, p_=pt, h_=head, sb_=sub: t.transpose(
                                            out=p_[:, sb_ * 65:(sb_ + 1) * 65],
                                            in_=of32[0:65, h_ * 512 + sb_ * 128: h_ * 512 + (sb_ + 1) * 128],
                                            identity=id32[0:65, 0:65]),
                                            wait=wl_tt,
                                            inc=te if (head == 1 and sub == 3) else None)
                                c_r0 = E("vector", lambda v: v.reciprocal(
                                    out=sca[:, 8:12],
                                    in_=ps_me[:, 0:260].rearrange("p (s c) -> p s c", c=65)[:, :, 64:65]),
                                    wait=[(te, c_tt)], inc=ve)
                                c_r1 = E("vector", lambda v: v.reciprocal(
                                    out=sca[:, 12:16],
                                    in_=ps_mo[:, 0:260].rearrange("p (s c) -> p s c", c=65)[:, :, 64:65]),
                                    wait=[(ve, c_r0)], inc=ve)

                                def make_tail(hp_=hp, ch_=ch, c_r1_=c_r1,
                                              c_tb8_prev=prev["c_tb8"]):
                                    def tail_fn():
                                        norm_se = None
                                        for head in range(2):
                                            pt = ps_me if head == 0 else ps_mo
                                            for sub in range(4):
                                                wl_sc = ()
                                                if head == 0 and sub == 0:
                                                    wl_sc = [(ve, c_r1_)]
                                                norm_se = E("scalar", lambda sc, p_=pt, h_=head, sb_=sub: sc.activation(
                                                    out=otm[:, h_ * 256 + sb_ * 64: h_ * 256 + (sb_ + 1) * 64],
                                                    in_=p_[:, sb_ * 65: sb_ * 65 + 64],
                                                    func=AF.Copy,
                                                    scale=sca[:, 8 + h_ * 4 + sb_:8 + h_ * 4 + sb_ + 1]),
                                                    wait=wl_sc,
                                                    inc=se if (head == 1 and sub == 3) else None)
                                        c_tb = None
                                        for head in range(2):
                                            tb = tb_e if head == 0 else tb_o
                                            for sub in range(4):
                                                wl_tb = ()
                                                if head == 0 and sub == 0:
                                                    wl_tb = [(se, norm_se), (ve, c_r1_)]
                                                c_tb = E("tensor", lambda t, tb_=tb, h_=head, sb_=sub: t.transpose(
                                                    out=tb_[0:64, sb_ * 128:(sb_ + 1) * 128],
                                                    in_=otm[:, h_ * 256 + sb_ * 64: h_ * 256 + (sb_ + 1) * 64],
                                                    identity=id16[:]),
                                                    wait=wl_tb,
                                                    inc=te if (head == 1 and sub == 3) else None)
                                        E("vector", lambda v, h_=hp_, c_=ch_: v.tensor_copy(
                                            out=Ofm[0:64, h_ * S + c_ * 512: h_ * S + (c_ + 1) * 512],
                                            in_=tb_e[0:64, 0:512]),
                                            wait=[(te, c_tb)])
                                        c_vo = E("vector", lambda v, h_=hp_, c_=ch_: v.tensor_copy(
                                            out=Ofm[64:128, h_ * S + c_ * 512: h_ * S + (c_ + 1) * 512],
                                            in_=tb_o[0:64, 0:512]),
                                            inc=ve)
                                        prev["norm_se"] = norm_se
                                        prev["c_tb8"] = c_tb
                                        prev["vofm"] = c_vo
                                    return tail_fn

                                tail[0] = make_tail()
                                prev["c_c1"] = c_c1
                                prev["c_tt8"] = c_tt
                                prev["c_r1"] = c_r1
                                prev["last_exp"] = last_exp
                                prev["last_pv"] = last_pv
                        tail[0]()
                        tail[0] = None
                        phaseB = {"ve": ve.n, "se": se.n, "te": te.n,
                                  "vofm": prev["vofm"]}


                    with ExitStack() as scx:
                        ec = scx.enter_context
                        xr = [ec(nc.sbuf_tensor(f"xr{s_}_{k}", [128, D], F32)) for k in range(2)]
                        pre_ln = ec(nc.sbuf_tensor("preln" + sfx, [128, D], F32))
                        ytile = ec(nc.sbuf_tensor("ytile" + sfx, [128, D], F32))
                        ps_c = [ec(nc.psum_tensor(f"ps_c{s_}_{k}", [128, 512], F32)) for k in range(4)]

                        c_resid = {}
                        c_yst = {}
                        for it in range(NT):
                            par = it % 2
                            wl_x = [(ve, c_resid.get(it - 2))]
                            if it < 2:
                                wl_x += [(ve, phaseB["ve"]), (te, phaseB["te"]),
                                         (se, phaseB["se"])]
                            E("gpsimd", lambda g, it_=it, p_=par, sq=s_: g.dma_start(
                                out=xr[p_][:], in_=x_in[sq, it_ * 128:(it_ + 1) * 128, :]),
                                wait=wl_x, inc=ldr[par], inc_n=16)
                            nldr = ldr[par].n
                            c_mm2 = {}
                            for oc in range(2):
                                bank = 2 * par + oc
                                wl = [(g_wout, wout_ready), (ve, phaseB["vofm"]),
                                      (ve, c_resid.get(it - 2))]
                                if it < 2:
                                    wl += [(se, phaseB["se"])]
                                E("tensor", lambda t, b_=bank, o_=oc: t.matmul(
                                    out=ps_c[b_][:], lhsT=ones16[:],
                                    rhs=bout16[0:1, o_ * 512:(o_ + 1) * 512],
                                    start=True, stop=False, skip_group_check=True),
                                    wait=wl)
                                c_mmx = None
                                for pc in range(DC):
                                    c_mmx = E("tensor", lambda t, b_=bank, o_=oc, p2=pc, it_=it: t.matmul(
                                        out=ps_c[b_][:],
                                        lhsT=Ofm[:, p2 * S + it_ * 128: p2 * S + it_ * 128 + 128],
                                        rhs=w_out16[:, p2 * D + o_ * 512: p2 * D + (o_ + 1) * 512],
                                        start=False, stop=(p2 == DC - 1), skip_group_check=True),
                                        inc=te if pc == DC - 1 else None)
                                c_mm2[oc] = c_mmx
                            c_ra = None
                            for oc in range(2):
                                bank = 2 * par + oc
                                wl_r = [(te, c_mm2[oc])]
                                if oc == 0:
                                    wl_r.append((ldr[par], nldr))
                                c_ra = E("vector", lambda v, b_=bank, o_=oc, p_=par: v.tensor_tensor(
                                    out=pre_ln[:, o_ * 512:(o_ + 1) * 512],
                                    in0=ps_c[b_][:], in1=xr[p_][:, o_ * 512:(o_ + 1) * 512],
                                    op=ALU.add),
                                    wait=wl_r, inc=ve if oc == 1 else None)
                            c_resid[it] = c_ra
                            c_b0 = E("vector", lambda v: v.bn_stats(
                                out=sca[:, 24:30], in_=pre_ln[:, 0:512]),
                                wait=[(ve, c_ra)], inc=ve)
                            c_b1x = E("vector", lambda v: v.bn_stats(
                                out=sca[:, 30:36], in_=pre_ln[:, 512:1024]),
                                wait=[(ve, c_b0)], inc=ve)
                            c_ag = E("vector", lambda v: v.bn_aggr(
                                out=sca[:, 4:6], in_=sca[:, 24:36]),
                                wait=[(ve, c_b1x)], inc=ve)
                            c_t0 = E("vector", lambda v: v.tensor_scalar(
                                out=sca[:, 6:7], in0=sca[:, 5:6], scalar1=EPS,
                                scalar2=None, op0=ALU.add),
                                wait=[(ve, c_ag)], inc=ve)
                            c_sq = E("scalar", lambda sc: sc.activation(
                                out=sca[:, 6:7], in_=sca[:, 6:7], func=AF.Sqrt),
                                wait=[(ve, c_t0)], inc=se)
                            c_rc = E("vector", lambda v: v.reciprocal(
                                out=sca[:, 7:8], in_=sca[:, 6:7]),
                                wait=[(se, c_sq)], inc=ve)
                            c_y = E("vector", lambda v: v.tensor_scalar(
                                out=ytile[:], in0=pre_ln[:],
                                scalar1=sca[:, 4:5], scalar2=sca[:, 7:8],
                                op0=ALU.subtract, op1=ALU.mult),
                                wait=[(ve, c_rc), (st_y, c_yst.get(it - 1))], inc=ve)
                            c_ys = E("sync", lambda s, it_=it: s.dma_start(
                                out=y_s[it_ * 128:(it_ + 1) * 128, :], in_=ytile[:]),
                                wait=[(ve, c_y)], inc=st_y, inc_n=16)
                            c_yst[it] = c_ys
                        phaseC = {"ve": ve.n, "se": se.n, "te": te.n,
                                  "st_y": st_y.n}

            with ExitStack() as sd:
                ed = sd.enter_context
                wq1t = ed(nc.sbuf_tensor("wq1t" + sfx, [128, DC * DFF], BF16))
                wq2t = ed(nc.sbuf_tensor("wq2t" + sfx, [128, FC * D], BF16))
                ytm = ed(nc.sbuf_tensor("ytm" + sfx, [128, 2 * D], BF16))
                yfm = ed(nc.sbuf_tensor("yfm" + sfx, [128, DC * 512], BF16))
                h16 = ed(nc.sbuf_tensor("h16" + sfx, [128, FC * 512], BF16))
                yr = [ed(nc.sbuf_tensor(f"yr{s_}_{k}", [128, D], F32)) for k in range(2)]
                outt = ed(nc.sbuf_tensor("outt" + sfx, [128, D], F32))
                outf = ed(nc.sbuf_tensor("outf" + sfx, [128, D], F32))
                ps_h = [ed(nc.psum_tensor(f"ps_h{s_}_{k}", [128, 512], F32)) for k in range(2)]
                ps_f = [ed(nc.psum_tensor(f"ps_f{s_}_{k}", [128, 512], F32)) for k in range(4)]
                ps_y = [ed(nc.psum_tensor(f"ps_y{s_}_{k}", [128, D], BF16)) for k in range(2)]

                phC = [(te, phaseC["te"]), (se, phaseC["se"]), (ve, phaseC["ve"])]
                E("gpsimd", lambda g: g.dma_start(out=wq1t[:], in_=wq1t_s),
                  wait=[(st_wq, quant_done_st)] + phC, inc=g_wq, inc_n=16)
                E("gpsimd", lambda g: g.dma_start(out=wq2t[:], in_=wq2t_s),
                  inc=g_wq, inc_n=16)
                wq_ready = g_wq.n

                c_out_prev = {}
                c_yfm = {}
                c_tylast = {}
                c_res2 = {}
                c_ost = {}
                for chn in range(NCH):
                    c_yf = None
                    c_ty = None
                    c_ty_half = {}
                    for t4 in range(4):
                        pyp = t4 % 2
                        prev_yf = c_yf
                        if t4 % 2 == 0:
                            half = t4 // 2
                            wl_yl = [(st_y, phaseC["st_y"])]
                            if chn == 0 and half == 0:
                                wl_yl += phC
                            if half == 1:
                                wl_yl.append((te, c_ty_half[0]))
                            elif chn > 0:
                                wl_yl.append((te, c_tylast[chn - 1]))
                            c_yl = E("gpsimd", lambda g, c_=chn, h_=half: g.dma_start(
                                out=ytm[:].rearrange("p (t d) -> p t d", t=2),
                                in_=y_s[c_ * 512 + h_ * 256: c_ * 512 + (h_ + 1) * 256, :].rearrange(
                                    "(t p) d -> p t d", p=128),
                                max_dma_last_dim=8192),
                                wait=wl_yl, inc=g_ytm, inc_n=16)
                        for dc in range(DC):
                            wl_ty = ()
                            if dc == 0:
                                wl_ty = [(g_ytm, c_yl)]
                                if t4 >= 2:
                                    wl_ty.append((ve, prev_yf))
                                if t4 == 0 and chn > 0:
                                    wl_ty.append((ve, c_yfm[chn - 1]))
                                    wl_ty.append((te, c_out_prev[chn - 1]))
                            c_ty = E("tensor", lambda t, p_=pyp, t_=t4, d_=dc: t.transpose(
                                out=ps_y[p_][:, d_ * 128:(d_ + 1) * 128],
                                in_=ytm[:, (t_ % 2) * D + d_ * 128: (t_ % 2) * D + (d_ + 1) * 128],
                                identity=id16[:]),
                                wait=wl_ty, inc=te if dc == DC - 1 else None)
                        if t4 % 2 == 1:
                            c_ty_half[t4 // 2] = c_ty
                        c_yf = E("vector", lambda v, p_=pyp, t_=t4: v.tensor_copy(
                            out=yfm[:].rearrange("p (c n) -> p c n", c=DC)[:, :, t_ * 128:(t_ + 1) * 128],
                            in_=ps_y[p_][:].rearrange("p (c n) -> p c n", c=DC)),
                            wait=[(te, c_ty)], inc=ve)
                    c_yfm[chn] = c_yf
                    c_tylast[chn] = c_ty

                    c_relu = {}
                    for fc in range(FC):
                        hpar = fc % 2
                        c_m = None
                        for dc in range(DC):
                            wl = ()
                            if dc == 0:
                                wl = [(ve, c_yf), (g_wq, wq_ready)]
                                prev = c_relu.get(fc - 2)
                                if prev is not None:
                                    wl.append(prev)
                                if chn > 0 and fc < 2:
                                    wl.append((te, c_out_prev[chn - 1]))
                            c_m = E("tensor", lambda t, p_=hpar, f_=fc, d_=dc: t.matmul(
                                out=ps_h[p_][:],
                                lhsT=wq1t[:, d_ * DFF + f_ * 128: d_ * DFF + (f_ + 1) * 128],
                                rhs=yfm[:, d_ * 512:(d_ + 1) * 512],
                                start=(d_ == 0), stop=(d_ == DC - 1), skip_group_check=True),
                                wait=wl, inc=te if dc == DC - 1 else None)
                        if fc % 2 == 0:
                            c_r = E("vector", lambda v, p_=hpar, f_=fc: v.tensor_scalar(
                                out=h16[:, f_ * 512:(f_ + 1) * 512], in0=ps_h[p_][:],
                                scalar1=b1T[:, f_:f_ + 1], scalar2=0.0,
                                op0=ALU.add, op1=ALU.max),
                                wait=[(te, c_m)], inc=ve)
                            c_relu[fc] = (ve, c_r)
                        else:
                            c_r = E("scalar", lambda sc, p_=hpar, f_=fc: sc.activation(
                                out=h16[:, f_ * 512:(f_ + 1) * 512], in_=ps_h[p_][:],
                                func=AF.Relu, bias=b1T[:, f_:f_ + 1]),
                                wait=[(te, c_m)], inc=se)
                            c_relu[fc] = (se, c_r)
                    h_ready = [c_relu[FC - 2], c_relu[FC - 1]]

                    c_mm2l = None
                    for it4 in range(4):
                        it = chn * 4 + it4
                        par = it4 % 2
                        wl_y2 = [(st_y, phaseC["st_y"]), (ve, c_res2.get(it - 2))]
                        if it < 2:
                            wl_y2 += phC
                        E("sync", lambda sy, it_=it, p_=par: sy.dma_start(
                            out=yr[p_][:], in_=y_s[it_ * 128:(it_ + 1) * 128, :]),
                            wait=wl_y2, inc=ldr[par], inc_n=16)
                        nldr2 = ldr[par].n
                        c_mm2 = {}
                        for oc in range(2):
                            bank = 2 * par + oc
                            wl = list(h_ready) + [(ve, c_res2.get(it - 2))]
                            E("tensor", lambda t, b_=bank, o_=oc: t.matmul(
                                out=ps_f[b_][:], lhsT=ones16[:],
                                rhs=b2_16[0:1, o_ * 512:(o_ + 1) * 512],
                                start=True, stop=False, skip_group_check=True),
                                wait=wl)
                            c_q = None
                            for fc in range(FC):
                                c_q = E("tensor", lambda t, b_=bank, o_=oc, f_=fc, i_=it4: t.matmul(
                                    out=ps_f[b_][:],
                                    lhsT=h16[:, f_ * 512 + i_ * 128: f_ * 512 + i_ * 128 + 128],
                                    rhs=wq2t[:, f_ * D + o_ * 512: f_ * D + (o_ + 1) * 512],
                                    start=False, stop=(f_ == FC - 1), skip_group_check=True),
                                    inc=te if fc == FC - 1 else None)
                            c_mm2[oc] = c_q
                        c_mm2l = c_mm2[1]
                        c_ra = None
                        for oc in range(2):
                            bank = 2 * par + oc
                            wl_r = [(te, c_mm2[oc])]
                            if oc == 0:
                                wl_r.append((ldr[par], nldr2))
                            c_ra = E("vector", lambda v, b_=bank, o_=oc, p_=par: v.tensor_tensor(
                                out=outt[:, o_ * 512:(o_ + 1) * 512],
                                in0=ps_f[b_][:], in1=yr[p_][:, o_ * 512:(o_ + 1) * 512],
                                op=ALU.add),
                                wait=wl_r, inc=ve if oc == 1 else None)
                        c_b0 = E("vector", lambda v: v.bn_stats(
                            out=sca[:, 24:30], in_=outt[:, 0:512]),
                            wait=[(ve, c_ra)], inc=ve)
                        c_b1x = E("vector", lambda v: v.bn_stats(
                            out=sca[:, 30:36], in_=outt[:, 512:1024]),
                            wait=[(ve, c_b0)], inc=ve)
                        c_ag = E("vector", lambda v: v.bn_aggr(
                            out=sca[:, 4:6], in_=sca[:, 24:36]),
                            wait=[(ve, c_b1x)], inc=ve)
                        c_t0 = E("vector", lambda v: v.tensor_scalar(
                            out=sca[:, 6:7], in0=sca[:, 5:6], scalar1=EPS,
                            scalar2=None, op0=ALU.add),
                            wait=[(ve, c_ag)], inc=ve)
                        c_sq = E("scalar", lambda sc: sc.activation(
                            out=sca[:, 6:7], in_=sca[:, 6:7], func=AF.Sqrt),
                            wait=[(ve, c_t0)], inc=se)
                        c_rc = E("vector", lambda v: v.reciprocal(
                            out=sca[:, 7:8], in_=sca[:, 6:7]),
                            wait=[(se, c_sq)], inc=ve)
                        c_o = E("vector", lambda v: v.tensor_scalar(
                            out=outf[:], in0=outt[:],
                            scalar1=sca[:, 4:5], scalar2=sca[:, 7:8],
                            op0=ALU.subtract, op1=ALU.mult),
                            wait=[(ve, c_rc), (st_out, c_ost.get(it - 1))], inc=ve)
                        c_res2[it] = c_o
                        c_ost[it] = E("sync", lambda s, it_=it, sq=s_: s.dma_start(
                            out=out[sq, it_ * 128:(it_ + 1) * 128, :], in_=outf[:]),
                            wait=[(ve, c_o)], inc=st_out, inc_n=16)
                    c_out_prev[chn] = c_mm2l
                prev_end = {"ve": ve.n, "se": se.n, "te": te.n,
                            "st_wq": quant_done_st}

        E("sync", lambda s: s.nop(), wait=[(st_out, st_out.n)])

        with nc.Block() as block:
            P.realize(block)

    return nc



def _consts():
    i = np.arange(128)[:, None]
    c = np.arange(512)[None, :]
    masks = []
    for k in range(4):
        m = ((128 * k + i) >= c).astype(np.float32)
        masks += [m, m]
    cmask = np.concatenate(masks, axis=1).astype(ml_dtypes.bfloat16)
    id32 = np.eye(128, dtype=np.float32)
    id16 = np.eye(128, dtype=np.float32).astype(ml_dtypes.bfloat16)
    return cmask, id32, id16


def make_in_map(xshard, w_qkv, b_qkv, w_out, b_out, W1, b1, W2, b2):
    cmask, id32, id16 = _consts()
    return {
        "x": np.ascontiguousarray(xshard, dtype=np.float32),
        "w_qkv": np.ascontiguousarray(w_qkv, dtype=np.float32),
        "w_out": np.ascontiguousarray(w_out, dtype=np.float32),
        "W1": np.ascontiguousarray(W1, dtype=np.float32),
        "W2": np.ascontiguousarray(W2, dtype=np.float32),
        "bqkv_blk": np.ascontiguousarray(
            np.asarray(b_qkv, dtype=np.float32).reshape(24, 128)),
        "bqkv_row": np.ascontiguousarray(
            np.asarray(b_qkv, dtype=np.float32).reshape(1, 3 * D)),
        "b1_blk": np.ascontiguousarray(
            np.asarray(b1, dtype=np.float32).reshape(32, 128)),
        "bout_row": np.ascontiguousarray(
            np.asarray(b_out, dtype=np.float32).reshape(1, D)),
        "b2_row": np.ascontiguousarray(
            np.asarray(b2, dtype=np.float32).reshape(1, D)),
        "cmask": cmask, "id32": id32, "id16": id16,
    }


def kernel(L0_attn_x, L0_attn_w_qkv, L0_attn_b_qkv, L0_attn_w_out,
           L0_attn_b_out, L0_ffn_W1, L0_ffn_b1, L0_ffn_W2, L0_ffn_b2):
    x = np.ascontiguousarray(np.asarray(L0_attn_x, dtype=np.float32))
    B = x.shape[0]
    per = B // NCORES
    nc = build_core_program(per)
    in_maps = []
    for c in range(NCORES):
        in_maps.append(make_in_map(
            x[c * per:(c + 1) * per], L0_attn_w_qkv, L0_attn_b_qkv,
            L0_attn_w_out, L0_attn_b_out, L0_ffn_W1, L0_ffn_b1,
            L0_ffn_W2, L0_ffn_b2))
    res = run_bass_kernel_spmd(nc, in_maps, core_ids=list(range(NCORES)))
    return np.concatenate([np.asarray(r["out"]) for r in res.results], axis=0)
